# version 02
# baseline (speedup 1.0000x reference)
import numpy as np
import concourse.bass as bass
import concourse.mybir as mybir
from concourse.bass_utils import run_bass_kernel_spmd

F32 = mybir.dt.float32
BF16 = mybir.dt.bfloat16
AF = mybir.ActivationFunctionType
ALU = mybir.AluOpType
AX = mybir.AxisListType

import os
B, S, D, FF = 16, 4096, 1024, 4096
NCORE = 8
BC = B // NCORE
T = 512
NCH = BC * S // T
NCH = int(os.environ.get("KERNEL_NCH", NCH))
PHASE = int(os.environ.get("KERNEL_PHASE", 99))
SUB = int(os.environ.get("KERNEL_SUB", 1023))
DC = D // 128
FT = FF // 128
TT = T // 128
EPS = 1e-5

PI = float(np.float32(np.pi))
TWO_PI_HI = float(np.float32(2 * np.pi))
TWO_PI_LO = float(np.float32(2 * np.pi - np.float64(np.float32(2 * np.pi))))
PI_HI = float(np.float32(np.pi))
PI_LO = float(np.float32(np.pi - np.float64(np.float32(np.pi))))
HPI = float(np.float32(np.pi / 2))
HPI_HI = HPI
HPI_LO = float(np.float32(np.pi / 2 - np.float64(np.float32(np.pi / 2))))

ENGSEM = {"tensor": "pe", "vector": "dv", "scalar": "ac"}


def build_program():
    nc = bass.Bass("TRN2", target_bir_lowering=False, detect_race_conditions=False)

    d_x = nc.dram_tensor("x", [BC, S, D], F32, kind="ExternalInput").ap()
    d_xT = nc.dram_tensor("xT", [BC, D, S], F32, kind="ExternalInput").ap()
    d_waT = nc.dram_tensor("waT", [D, D], F32, kind="ExternalInput").ap()
    d_wxT = nc.dram_tensor("wxT", [D, D], F32, kind="ExternalInput").ap()
    d_w1 = nc.dram_tensor("w1", [D, FF], F32, kind="ExternalInput").ap()
    d_w2 = nc.dram_tensor("w2", [FF, D], F32, kind="ExternalInput").ap()
    d_vecs = {}
    for nme in ["vba", "vbx", "vb2", "vlam"]:
        d_vecs[nme] = nc.dram_tensor(nme, [128, DC], F32, kind="ExternalInput").ap()
    for nme in ["vb1", "vth", "vwr", "vwi"]:
        d_vecs[nme] = nc.dram_tensor(nme, [128, FT], F32, kind="ExternalInput").ap()
    d_vecs["iden"] = nc.dram_tensor("iden", [128, 128], F32, kind="ExternalInput").ap()
    d_out = nc.dram_tensor("outT", [BC, D, S], F32, kind="ExternalOutput").ap()
    d_w1s = nc.dram_tensor("w1s", [FT, D, 128], BF16, kind="Internal").ap()
    d_w2s = nc.dram_tensor("w2s", [FF, D], BF16, kind="Internal").ap()

    import contextlib
    stack = contextlib.ExitStack()
    sb = {}

    def S_(name, shape, dtype=F32):
        sb[name] = stack.enter_context(nc.sbuf_tensor("sb_" + name, list(shape), dtype))

    S_("wa", (128, DC * 1024), BF16)
    S_("wx", (128, DC * 1024), BF16)
    S_("w1b", (128, 3 * 1024), BF16)
    S_("w2b", (128, 3 * 1024), BF16)
    S_("xt", (128, DC * T))
    S_("xk", (128, 2 * 2048))
    S_("xn32", (128, DC * T))
    S_("xnbf", (128, DC * T), BF16)
    S_("s", (128, DC * T))
    S_("act", (128, FT * T), BF16)
    S_("it", (128, 4 * T))
    S_("a", (128, 4 * T))
    S_("asq", (128, 4 * T))
    S_("sq", (128, 4 * T))
    S_("s1", (128, 2 * T))
    S_("q", (128, 2 * T))
    S_("g", (128, 2 * T))
    S_("outb", (128, 2 * T))
    S_("sqs", (128, 512))
    S_("strow", (1, 1024))
    S_("ssq8", (128, 2 * TT))
    for nme in ["ssum", "ssq", "mean", "var", "var2", "std", "rstd", "mr"]:
        S_(nme, (128, TT))
    for nme in ["vba", "vbx", "vb2", "vlam", "coefh", "ba2", "bx2"]:
        S_(nme, (128, DC))
    for nme in ["vb1", "vth", "vwr", "vwi", "phi", "sgn", "V", "sinb",
                "u1", "u2", "u3", "u4"]:
        S_(nme, (128, FT))
    S_("iden", (128, 128))
    S_("ones", (1, 128))
    S_("carry", (128, DC))

    act_f32 = sb["act"].ap().bitcast(F32)

    def stage_ap(sl):
        return act_f32[:, 6144 + sl * 1024: 6144 + (sl + 1) * 1024]

    ps = [stack.enter_context(nc.psum_tensor(f"ps{i}", [128, T], F32))
          for i in range(8)]
    PS_RT, PS_IT, PS_H0, PS_H1 = ps[0], ps[1], ps[2], ps[3]
    PS_F = ps[4:8]

    semnames = (["pe", "dv", "ac", "mis", "scrA", "scrB"]
                + ["xt"] + [f"xk{k}" for k in range(2)]
                + [f"w1_{k}" for k in range(3)] + [f"w2_{k}" for k in range(3)]
                + [f"st{k}" for k in range(2)] + [f"ou{k}" for k in range(2)])
    sems = {n: stack.enter_context(nc.semaphore(f"s_{n}")) for n in semnames}

    prog = {e: [] for e in ["sync", "tensor", "vector", "scalar", "gpsimd"]}
    cnt = {"pe": 0, "dv": 0, "ac": 0}
    dcnt = {n: 0 for n in semnames if n not in cnt}
    last_wait = {e: {} for e in prog}

    def W(eng, tok):
        if tok is None:
            return
        if isinstance(tok, (list, tuple)) and tok and isinstance(tok[0], (list, tuple, type(None))):
            for t in tok:
                W(eng, t)
            return
        key, val = tok
        if val <= 0:
            return
        lw = last_wait[eng]
        if lw.get(key, 0) >= val:
            return
        lw[key] = val
        sem = sems[key]
        prog[eng].append(lambda e, sem=sem, val=val: e.wait_ge(sem, val))

    def OP(eng, fn, waits=(), inc=True):
        for t in waits:
            W(eng, t)
        if inc:
            key = ENGSEM[eng]
            cnt[key] += 1
            tok = (key, cnt[key])
            sem = sems[key]
            prog[eng].append(lambda e, fn=fn, sem=sem: fn(e).then_inc(sem, 1))
            return tok
        prog[eng].append(lambda e, fn=fn: fn(e))
        return None

    def DMA(eng, fn, semkey, waits=()):
        for t in waits:
            W(eng, t)
        dcnt[semkey] += 16
        tok = (semkey, dcnt[semkey])
        sem = sems[semkey]
        prog[eng].append(lambda e, fn=fn, sem=sem: fn(e).then_inc(sem, 16))
        return tok

    ring = {
        "xkslot": [None, None],
        "it": [None] * 4, "a": [None] * 4, "asq": [None] * 4,
        "sq": [None] * 4, "s1": [None, None],
        "q": [None, None], "g": [None, None],
        "xt_ready": [None] * DC, "xk_ready": [None, None],
        "psrt": [None], "psit": [None], "bc": [None],
        "xn": [None] * DC,
    }
    w1_cons = {}
    w2_cons = {}
    ou_occ = [0, 0]
    state = {"w1_g": 0, "w2_g": 0}

    mis_toks = []
    for nme in ["iden", "vba", "vbx", "vb2", "vlam", "vb1", "vth", "vwr", "vwi"]:
        src = d_vecs[nme]
        mis_toks.append(DMA("sync", lambda e, s2=src, t2=sb[nme]:
                            e.dma_start(out=t2[:, :], in_=s2), "mis"))
    tok_mis = ("mis", dcnt["mis"])

    stage_jobs = []
    if PHASE < 2:
        global_stage_skip = True
    for dc in range(DC):
        stage_jobs.append(("wa", d_waT[dc * 128:(dc + 1) * 128, :], dc))
    for dc in range(DC):
        stage_jobs.append(("wx", d_wxT[dc * 128:(dc + 1) * 128, :], dc))
    for dc in range(DC):
        for qq in range(4):
            stage_jobs.append(("w1", d_w1[dc * 128:(dc + 1) * 128,
                                          qq * 1024:(qq + 1) * 1024], (dc, qq)))
    for fc in range(FT):
        stage_jobs.append(("w2", d_w2[fc * 128:(fc + 1) * 128, :], fc))

    stg_cons = [None, None]
    bfw1_cons = [None, None]
    bfw2_cons = [None, None]
    w1_wr_last = [None, None]
    w2_wr_last = [None, None]
    tok_wa_done = None
    if PHASE < 2:
        stage_jobs = []
    for idx, (kind, src, meta) in enumerate(stage_jobs):
        sl = idx % 2
        ld = DMA("sync", lambda e, s2=src, sl=sl:
                 e.dma_start(out=stage_ap(sl), in_=s2),
                 f"st{sl}", waits=[stg_cons[sl]])
        if kind in ("wa", "wx"):
            dst, dc = sb[kind], meta
            cv = OP("vector", lambda e, dst=dst, dc=dc, sl=sl:
                    e.tensor_copy(out=dst[:, dc * 1024:(dc + 1) * 1024],
                                  in_=stage_ap(sl)),
                    waits=[ld])
            stg_cons[sl] = cv
            if kind == "wx" and meta == DC - 1:
                tok_wa_done = cv
        else:
            bfbuf = sb["w1b"] if kind == "w1" else sb["w2b"]
            bcons = bfw1_cons if kind == "w1" else bfw2_cons
            cv = OP("vector", lambda e, sl=sl, bfbuf=bfbuf:
                    e.tensor_copy(out=bfbuf[:, sl * 1024:(sl + 1) * 1024],
                                  in_=stage_ap(sl)),
                    waits=[ld, bcons[sl]])
            stg_cons[sl] = cv
            if PHASE < 3:
                bcons[sl] = None
                continue
            sem = "scrA" if sl == 0 else "scrB"
            if kind == "w1":
                dc, qq = meta
                dstap = d_w1s[qq * 8:(qq + 1) * 8, dc * 128:(dc + 1) * 128, :]
                dstap = dstap.rearrange("ft p f -> p ft f")
                srcap = sb["w1b"][:, sl * 1024:(sl + 1) * 1024]
                srcap = srcap.rearrange("p (ft f) -> p ft f", f=128)
                wr = DMA("sync", lambda e, d2=dstap, s2=srcap:
                         e.dma_start(out=d2, in_=s2), sem, waits=[cv])
                w1_wr_last[sl] = wr
            else:
                fc = meta
                dstap = d_w2s[fc * 128:(fc + 1) * 128, :]
                srcap = sb["w2b"][:, sl * 1024:(sl + 1) * 1024]
                wr = DMA("sync", lambda e, d2=dstap, s2=srcap:
                         e.dma_start(out=d2, in_=s2), sem, waits=[cv])
                w2_wr_last[sl] = wr
            bcons[sl] = wr
    tok_w1s = [t for t in w1_wr_last if t]
    tok_w2s = [t for t in w2_wr_last if t]

    u1, u2, u3, u4 = sb["u1"], sb["u2"], sb["u3"], sb["u4"]
    if SUB & 1:
        t_ones = OP("vector", lambda e: e.memset(sb["ones"][:, :], 1.0))
        t1 = OP("vector", lambda e: e.tensor_scalar_mul(out=u1[:, :DC], in0=sb["vlam"][:, :], scalar1=-1.0), waits=[tok_mis])
        t2 = OP("vector", lambda e: e.tensor_scalar_max(out=u2[:, :DC], in0=u1[:, :DC], scalar1=0.0), waits=[t1])
        t3 = OP("scalar", lambda e: e.activation(out=u3[:, :DC], in_=sb["vlam"][:, :], func=AF.Abs), waits=[tok_mis])
        a1 = OP("scalar", lambda e: e.activation(out=u4[:, :DC], in_=u3[:, :DC], func=AF.Exp, scale=-1.0), waits=[t3])
        a2 = OP("scalar", lambda e: e.activation(out=u3[:, :DC], in_=u4[:, :DC], func=AF.Ln, bias=1.0), waits=[a1])
        t4 = OP("vector", lambda e: e.tensor_add(out=u1[:, :DC], in0=u2[:, :DC], in1=u3[:, :DC]), waits=[t2, a2])
        tok_coef = OP("vector", lambda e: e.tensor_scalar_mul(out=sb["coefh"][:, :], in0=u1[:, :DC], scalar1=-4.0), waits=[t4])
        tok_ba2 = OP("vector", lambda e: e.tensor_scalar_mul(out=sb["ba2"][:, :], in0=sb["vba"][:, :], scalar1=0.5), waits=[tok_mis])
        tok_bx2 = OP("vector", lambda e: e.tensor_scalar_mul(out=sb["bx2"][:, :], in0=sb["vbx"][:, :], scalar1=0.5), waits=[tok_mis])

        pt = OP("vector", lambda e: e.tensor_copy(out=sb["phi"][:, :], in_=sb["vth"][:, :]), waits=[tok_mis])
        for _ in range(6):
            m1 = OP("vector", lambda e: e.tensor_single_scalar(out=u1[:, :], in_=sb["phi"][:, :], scalar=PI, op=ALU.is_gt), waits=[pt])
            p1 = OP("vector", lambda e: e.scalar_tensor_tensor(out=sb["phi"][:, :], in0=u1[:, :], scalar=-TWO_PI_HI, in1=sb["phi"][:, :], op0=ALU.mult, op1=ALU.add), waits=[m1])
            p2 = OP("vector", lambda e: e.scalar_tensor_tensor(out=sb["phi"][:, :], in0=u1[:, :], scalar=-TWO_PI_LO, in1=sb["phi"][:, :], op0=ALU.mult, op1=ALU.add), waits=[p1])
            m2 = OP("vector", lambda e: e.tensor_single_scalar(out=u1[:, :], in_=sb["phi"][:, :], scalar=-PI, op=ALU.is_lt), waits=[p2])
            p3 = OP("vector", lambda e: e.scalar_tensor_tensor(out=sb["phi"][:, :], in0=u1[:, :], scalar=TWO_PI_HI, in1=sb["phi"][:, :], op0=ALU.mult, op1=ALU.add), waits=[m2])
            pt = OP("vector", lambda e: e.scalar_tensor_tensor(out=sb["phi"][:, :], in0=u1[:, :], scalar=TWO_PI_LO, in1=sb["phi"][:, :], op0=ALU.mult, op1=ALU.add), waits=[p3])
        mh = OP("vector", lambda e: e.tensor_single_scalar(out=u1[:, :], in_=sb["phi"][:, :], scalar=HPI, op=ALU.is_gt), waits=[pt])
        ml = OP("vector", lambda e: e.tensor_single_scalar(out=u2[:, :], in_=sb["phi"][:, :], scalar=-HPI, op=ALU.is_lt), waits=[mh])
        r1 = OP("vector", lambda e: e.scalar_tensor_tensor(out=sb["phi"][:, :], in0=u1[:, :], scalar=-PI_HI, in1=sb["phi"][:, :], op0=ALU.mult, op1=ALU.add), waits=[ml])
        r2 = OP("vector", lambda e: e.scalar_tensor_tensor(out=sb["phi"][:, :], in0=u1[:, :], scalar=-PI_LO, in1=sb["phi"][:, :], op0=ALU.mult, op1=ALU.add), waits=[r1])
        r3 = OP("vector", lambda e: e.scalar_tensor_tensor(out=sb["phi"][:, :], in0=u2[:, :], scalar=PI_HI, in1=sb["phi"][:, :], op0=ALU.mult, op1=ALU.add), waits=[r2])
        tok_phi = OP("vector", lambda e: e.scalar_tensor_tensor(out=sb["phi"][:, :], in0=u2[:, :], scalar=PI_LO, in1=sb["phi"][:, :], op0=ALU.mult, op1=ALU.add), waits=[r3])
        sg1 = OP("vector", lambda e: e.tensor_add(out=u3[:, :], in0=u1[:, :], in1=u2[:, :]), waits=[tok_phi])
        tok_sgn = OP("vector", lambda e: e.tensor_scalar(out=sb["sgn"][:, :], in0=u3[:, :], scalar1=-2.0, scalar2=1.0, op0=ALU.mult, op1=ALU.add), waits=[sg1])
        s_sin = OP("scalar", lambda e: e.activation(out=u1[:, :], in_=sb["phi"][:, :], func=AF.Sin), waits=[tok_phi, tok_sgn])
        c1 = OP("vector", lambda e: e.tensor_scalar_add(out=u2[:, :], in0=sb["phi"][:, :], scalar1=HPI_HI), waits=[tok_sgn])
        c2 = OP("vector", lambda e: e.tensor_scalar_add(out=u2[:, :], in0=u2[:, :], scalar1=HPI_LO), waits=[c1])
        s_cos = OP("scalar", lambda e: e.activation(out=u3[:, :], in_=u2[:, :], func=AF.Sin), waits=[c2])
        v1 = OP("vector", lambda e: e.tensor_mul(out=u2[:, :], in0=u3[:, :], in1=sb["vwr"][:, :]), waits=[s_cos])
        v2 = OP("vector", lambda e: e.tensor_mul(out=u4[:, :], in0=u1[:, :], in1=sb["vwi"][:, :]), waits=[s_sin, v1])
        v3 = OP("vector", lambda e: e.tensor_add(out=u2[:, :], in0=u2[:, :], in1=u4[:, :]), waits=[v2])
        v4 = OP("vector", lambda e: e.tensor_mul(out=u2[:, :], in0=u2[:, :], in1=sb["vwi"][:, :]), waits=[v3])
        tok_V = OP("vector", lambda e: e.tensor_scalar_mul(out=sb["V"][:, :], in0=u2[:, :], scalar1=0.1), waits=[v4])
        b11 = OP("vector", lambda e: e.tensor_scalar_mul(out=u4[:, :], in0=sb["vb1"][:, :], scalar1=0.1), waits=[v2])
        tok_sinb = OP("vector", lambda e: e.tensor_add(out=sb["sinb"][:, :], in0=sb["phi"][:, :], in1=u4[:, :]), waits=[b11])


    else:
        t_ones = tok_coef = tok_ba2 = tok_bx2 = tok_V = tok_sinb = None

    def chunk_bt(i):
        return i // (S // T), (i % (S // T)) * T

    def emit_xt_slice(i, dc, free_tok):
        b, t0 = chunk_bt(i)
        src = d_xT[b, dc * 128:(dc + 1) * 128, t0:t0 + T]
        dst = sb["xt"][:, dc * T:(dc + 1) * T]
        ring["xt_ready"][dc] = DMA(
            "sync", lambda e, s2=src, d2=dst: e.dma_start(out=d2, in_=s2),
            "xt", waits=[free_tok])

    def emit_xk_loads(i):
        b, t0 = chunk_bt(i)
        for h in range(2):
            src = d_x[b, t0 + h * 256:t0 + (h + 1) * 256, :]
            src = src.rearrange("(tt p) d2 -> p tt d2", p=128)
            dst = sb["xk"][:, h * 2048:(h + 1) * 2048]
            dst = dst.rearrange("p (tt d2) -> p tt d2", d2=D)
            ring["xk_ready"][h] = DMA(
                "sync", lambda e, s2=src, d2=dst: e.dma_start(out=d2, in_=s2),
                f"xk{h}", waits=[ring["xkslot"][h]])

    def emit_w1_load(g):
        if g >= NCH * FT:
            return
        ft = g % FT
        sl = g % 3
        src = d_w1s[ft].rearrange("(dc p) f -> p dc f", p=128)
        dst = sb["w1b"][:, sl * 1024:(sl + 1) * 1024]
        dst = dst.rearrange("p (dc f) -> p dc f", f=128)
        waits = [w1_cons.get(g - 3)] if g >= 3 else list(tok_w1s)
        DMA("sync", lambda e, s2=src, d2=dst: e.dma_start(out=d2, in_=s2),
            f"w1_{sl}", waits=waits)

    def emit_w2_load(g):
        if g >= NCH * 2 * FT:
            return
        fc = g % FT
        sl = g % 3
        src = d_w2s[fc * 128:(fc + 1) * 128, :]
        dst = sb["w2b"][:, sl * 1024:(sl + 1) * 1024]
        waits = [w2_cons.get(g - 3)] if g >= 3 else list(tok_w2s)
        DMA("sync", lambda e, s2=src, d2=dst: e.dma_start(out=d2, in_=s2),
            f"w2_{sl}", waits=waits)

    def emit_stats_pre(i):
        sq_toks = []
        rd_toks = []
        for tt in range(TT):
            sl = tt // 2
            col = (tt % 2) * 1024
            for hh in range(2):
                xkap = sb["xk"][:, sl * 2048 + col + hh * 512:
                               sl * 2048 + col + (hh + 1) * 512]
                sq_toks.append(OP(
                    "scalar", lambda e, x2=xkap, tt=tt, hh=hh:
                    e.activation(out=sb["sqs"][:, :], in_=x2, func=AF.Square,
                                 accum_out=sb["ssq8"][:, hh * TT + tt:hh * TT + tt + 1]),
                    waits=[ring["xk_ready"][sl]]))
            xkap = sb["xk"][:, sl * 2048 + col: sl * 2048 + col + 1024]
            rd_toks.append(OP(
                "vector", lambda e, x2=xkap, tt=tt:
                e.reduce_sum(out=sb["ssum"][:, tt:tt + 1], in_=x2, axis=AX.X),
                waits=[ring["xk_ready"][sl]]))
        ring["xkslot"][0] = [rd_toks[1], sq_toks[3]]
        ring["xkslot"][1] = [rd_toks[3], sq_toks[7]]
        mc = OP("vector", lambda e: e.tensor_add(
            out=sb["ssq"][:, :], in0=sb["ssq8"][:, 0:TT], in1=sb["ssq8"][:, TT:2 * TT]),
            waits=[sq_toks[-1]])
        m1 = OP("vector", lambda e: e.tensor_scalar_mul(out=sb["mean"][:, :], in0=sb["ssum"][:, :], scalar1=1.0 / D),
                waits=[rd_toks[-1]])
        m2 = OP("vector", lambda e: e.tensor_mul(out=sb["var"][:, :], in0=sb["mean"][:, :], in1=sb["mean"][:, :]),
                waits=[m1])
        m3 = OP("vector", lambda e: e.scalar_tensor_tensor(out=sb["var2"][:, :], in0=sb["ssq"][:, :], scalar=1.0 / D, in1=sb["var"][:, :], op0=ALU.mult, op1=ALU.subtract),
                waits=[m2, mc])
        m4 = OP("vector", lambda e: e.tensor_scalar_add(out=sb["var2"][:, :], in0=sb["var2"][:, :], scalar1=EPS),
                waits=[m3])
        return {"var": m4}

    def emit_stats_post(i, toks):
        a_std = OP("scalar", lambda e: e.activation(out=sb["std"][:, :], in_=sb["var2"][:, :], func=AF.Sqrt),
                   waits=[toks["var"]])
        d_rs = OP("vector", lambda e: e.reciprocal(out=sb["rstd"][:, :], in_=sb["std"][:, :]),
                  waits=[a_std])
        d_mr = OP("vector", lambda e: e.tensor_mul(out=sb["mr"][:, :], in0=sb["mean"][:, :], in1=sb["rstd"][:, :]),
                  waits=[d_rs])
        tr_toks = []
        for tt in range(TT):
            tr_toks.append(OP(
                "tensor", lambda e, tt=tt:
                e.transpose(out=PS_RT[0:1, tt * 128:(tt + 1) * 128],
                            in_=sb["rstd"][:, tt:tt + 1], identity=sb["iden"][:, :]),
                waits=[d_rs, ring["psrt"][0]]))
        for tt in range(TT):
            tr_toks.append(OP(
                "tensor", lambda e, tt=tt:
                e.transpose(out=PS_IT[0:1, tt * 128:(tt + 1) * 128],
                            in_=sb["mr"][:, tt:tt + 1], identity=sb["iden"][:, :]),
                waits=[d_mr, ring["psit"][0]]))
        srow_toks = []
        for tt in range(TT):
            srow_toks.append(OP(
                "vector", lambda e, tt=tt:
                e.tensor_copy(out=sb["strow"][0:1, tt * 128:(tt + 1) * 128],
                              in_=PS_RT[0:1, tt * 128:(tt + 1) * 128]),
                waits=[tr_toks[TT - 1]]))
        for tt in range(TT):
            srow_toks.append(OP(
                "vector", lambda e, tt=tt:
                e.tensor_copy(out=sb["strow"][0:1, 512 + tt * 128:512 + (tt + 1) * 128],
                              in_=PS_IT[0:1, tt * 128:(tt + 1) * 128]),
                waits=[tr_toks[2 * TT - 1]]))
        bc1 = OP("tensor", lambda e: e.matmul(out=PS_RT[:, :], lhsT=sb["ones"][:, :], rhs=sb["strow"][0:1, 0:512], start=True, stop=True),
                 waits=srow_toks[0:4] + [t_ones])
        bc2 = OP("tensor", lambda e: e.matmul(out=PS_IT[:, :], lhsT=sb["ones"][:, :], rhs=sb["strow"][0:1, 512:1024], start=True, stop=True),
                 waits=srow_toks[4:8])
        return bc1, bc2

    prev = {
        "xn": None, "combine": [None] * DC, "scan": [None] * DC,
        "carry": [None] * DC, "silu_bank": [None] * 4,
        "ffn1_last_pe": None, "ffn2B": {}, "pshfree": [None, None],
    }

    def emit_xn_dc(i, dc, bc_toks):
        bc1, bc2 = bc_toks
        xt_ap = sb["xt"][:, dc * T:(dc + 1) * T]
        w0 = [ring["xt_ready"][dc], bc1, bc2, ring["xn"][dc], prev["scan"][dc]]
        o1 = OP("vector", lambda e, x2=xt_ap, dc=dc:
                e.tensor_mul(out=sb["xn32"][:, dc * T:(dc + 1) * T], in0=x2, in1=PS_RT[:, :]),
                waits=w0)
        o2 = OP("vector", lambda e, dc=dc:
                e.tensor_sub(out=sb["xn32"][:, dc * T:(dc + 1) * T],
                             in0=sb["xn32"][:, dc * T:(dc + 1) * T], in1=PS_IT[:, :]),
                waits=[o1])
        o3 = OP("vector", lambda e, dc=dc:
                e.tensor_copy(out=sb["xnbf"][:, dc * T:(dc + 1) * T],
                              in_=sb["xn32"][:, dc * T:(dc + 1) * T]),
                waits=[o2])
        return (o1, o2, o3)

    if SUB & 8:
        for dc in range(DC):
            emit_xt_slice(0, dc, None)
        emit_xk_loads(0)
    if SUB & 4:
        for g in range(3):
            emit_w1_load(g)
            emit_w2_load(g)
    if SUB & 2:
        st0 = emit_stats_pre(0)
    if SUB & 16:
        bc0 = emit_stats_post(0, st0)
    if SUB & 32:
        _xn0 = [emit_xn_dc(0, dc, bc0) for dc in range(DC)]
        prev["xn"] = [(o2, o3) for (o1, o2, o3) in _xn0]
        ring["psrt"][0] = _xn0[-1][0]
        ring["psit"][0] = _xn0[-1][1]

    for i in range(NCH):
        b, t0 = chunk_bt(i)
        batch_start = (i % (S // T)) == 0

        if i + 1 < NCH:
            emit_xk_loads(i + 1)

        tok_a = [None] * DC
        stn = None
        for half in (range(2) if SUB & 64 else []):
            dts = list(range(half * 4, half * 4 + 4))
            tok_it = {}
            tok_cl = {}
            for dt in dts:
                asl = dt % 4
                w0 = [x[1] for x in prev["xn"]] + [ring["psrt"][0], tok_wa_done]
                tokr = None
                for dc in range(DC):
                    last = dc == DC - 1
                    tokr = OP("tensor", lambda e, dc=dc, dt=dt, last=last:
                              e.matmul(out=PS_RT[:, :],
                                       lhsT=sb["wa"][:, dc * 1024 + dt * 128: dc * 1024 + (dt + 1) * 128],
                                       rhs=sb["xnbf"][:, dc * T:(dc + 1) * T],
                                       start=(dc == 0), stop=last),
                              waits=(w0 if dc == 0 else ()), inc=last)
                pe_rt = tokr
                toki = None
                for dc in range(DC):
                    last = dc == DC - 1
                    toki = OP("tensor", lambda e, dc=dc, dt=dt, last=last:
                              e.matmul(out=PS_IT[:, :],
                                       lhsT=sb["wx"][:, dc * 1024 + dt * 128: dc * 1024 + (dt + 1) * 128],
                                       rhs=sb["xnbf"][:, dc * T:(dc + 1) * T],
                                       start=(dc == 0), stop=last),
                              waits=([ring["psit"][0]] if dc == 0 else ()), inc=last)
                pe_it = toki
                a_th = OP("scalar", lambda e, dt=dt:
                          e.activation(out=sb["sqs"][:, :], in_=PS_RT[:, :], func=AF.Tanh,
                                       scale=0.5, bias=sb["ba2"][:, dt:dt + 1]),
                          waits=[pe_rt, tok_ba2, tok_coef])
                ring["psrt"][0] = a_th
                a_a = OP("scalar", lambda e, dt=dt, asl=asl:
                         e.activation(out=sb["a"][:, asl * T:(asl + 1) * T], in_=sb["sqs"][:, :],
                                      func=AF.Exp, scale=sb["coefh"][:, dt:dt + 1],
                                      bias=sb["coefh"][:, dt:dt + 1]),
                         waits=[a_th, ring["a"][asl]])
                tok_a[dt] = a_a
                a_ti = OP("scalar", lambda e, dt=dt, asl=asl:
                          e.activation(out=sb["it"][:, asl * T:(asl + 1) * T], in_=PS_IT[:, :],
                                       func=AF.Tanh, scale=0.5, bias=sb["bx2"][:, dt:dt + 1]),
                          waits=[pe_it, tok_bx2, ring["it"][asl]])
                ring["psit"][0] = a_ti
                a_sq = OP("scalar", lambda e, asl=asl:
                          e.activation(out=sb["asq"][:, asl * T:(asl + 1) * T],
                                       in_=sb["a"][:, asl * T:(asl + 1) * T], func=AF.Square),
                          waits=[a_a, ring["asq"][asl]])
                d_it = OP("vector", lambda e, asl=asl:
                          e.tensor_scalar(out=sb["it"][:, asl * T:(asl + 1) * T],
                                          in0=sb["it"][:, asl * T:(asl + 1) * T],
                                          scalar1=0.5, scalar2=0.5, op0=ALU.mult, op1=ALU.add),
                          waits=[a_ti])
                tok_it[dt] = d_it
                tok_cl[dt] = OP("vector", lambda e, asl=asl:
                                e.tensor_single_scalar(out=sb["asq"][:, asl * T:(asl + 1) * T],
                                                       in_=sb["asq"][:, asl * T:(asl + 1) * T],
                                                       scalar=1.0, op=ALU.min),
                                waits=[a_sq])

            if half == 1 and i + 1 < NCH:
                stn = emit_stats_pre(i + 1)

            tok_s = {}
            for dt in dts:
                asl = dt % 4
                tok_s[dt] = OP("scalar", lambda e, asl=asl:
                               e.activation(out=sb["sq"][:, asl * T:(asl + 1) * T],
                                            in_=sb["asq"][:, asl * T:(asl + 1) * T],
                                            func=AF.Sqrt, scale=-1.0, bias=1.0),
                               waits=[tok_cl[dt], ring["sq"][asl]])
            if half == 1 and i + 1 < NCH:
                bcn = emit_stats_post(i + 1, stn)

            for dt in dts:
                asl = dt % 4
                d_t2 = OP("vector", lambda e, asl=asl, dt=dt:
                          e.tensor_mul(out=sb["asq"][:, asl * T:(asl + 1) * T],
                                       in0=sb["sq"][:, asl * T:(asl + 1) * T],
                                       in1=sb["xn32"][:, dt * T:(dt + 1) * T]),
                          waits=[tok_s[dt], prev["xn"][dt][0]])
                ring["sq"][asl] = d_t2
                d_b = OP("vector", lambda e, asl=asl:
                         e.tensor_mul(out=sb["asq"][:, asl * T:(asl + 1) * T],
                                      in0=sb["it"][:, asl * T:(asl + 1) * T],
                                      in1=sb["asq"][:, asl * T:(asl + 1) * T]),
                         waits=[d_t2, tok_it[dt]])
                ring["it"][asl] = d_b
                init = 0.0 if batch_start else sb["carry"][:, dt:dt + 1]
                d_sc = OP("vector", lambda e, asl=asl, dt=dt, init=init:
                          e.tensor_tensor_scan(out=sb["s"][:, dt * T:(dt + 1) * T],
                                               data0=sb["a"][:, asl * T:(asl + 1) * T],
                                               data1=sb["asq"][:, asl * T:(asl + 1) * T],
                                               initial=init, op0=ALU.mult, op1=ALU.add),
                          waits=[d_b, tok_a[dt], prev["combine"][dt], prev["carry"][dt]])
                ring["a"][asl] = d_sc
                ring["asq"][asl] = d_sc
                prev["scan"][dt] = d_sc
                prev["carry"][dt] = OP(
                    "vector", lambda e, dt=dt:
                    e.tensor_copy(out=sb["carry"][:, dt:dt + 1],
                                  in_=sb["s"][:, dt * T + T - 1: dt * T + T]),
                    waits=[d_sc])

        if not (SUB & 64) and i + 1 < NCH:
            stn = emit_stats_pre(i + 1)
            bcn = emit_stats_post(i + 1, stn)

        if not (SUB & 128):
            prev["ffn1_last_pe"] = None
            tok_act = [None] * FT
            sin_toks = [None] * FT
            q_toks = [None] * FT
        sin_toks = [None] * FT
        q_toks = [None] * FT
        tok_act = [None] * FT
        for ft in (range(FT) if SUB & 128 else []):
            g = i * FT + ft
            sl = g % 3
            psh = PS_H0 if ft % 2 == 0 else PS_H1
            w1_ready = (f"w1_{sl}", (g // 3 + 1) * 16)
            if ft >= 2:
                bankfree = [sin_toks[ft - 2], q_toks[ft - 2]]
            else:
                bankfree = prev["pshfree"][ft % 2] or []
            tokh = None
            for dc in range(DC):
                last = dc == DC - 1
                tokh = OP("tensor", lambda e, dc=dc, sl=sl, psh=psh, last=last:
                          e.matmul(out=psh[:, :],
                                   lhsT=sb["w1b"][:, sl * 1024 + dc * 128: sl * 1024 + (dc + 1) * 128],
                                   rhs=sb["xnbf"][:, dc * T:(dc + 1) * T],
                                   start=(dc == 0), stop=last),
                          waits=(([w1_ready] + list(bankfree)) if dc == 0 else ()),
                          inc=last)
            w1_cons[g] = tokh
            emit_w1_load(g + 3)
            s1sl = ft % 2
            a_sin = OP("scalar", lambda e, ft=ft, psh=psh, s1sl=s1sl:
                       e.activation(out=sb["s1"][:, s1sl * T:(s1sl + 1) * T], in_=psh[:, :],
                                    func=AF.Sin, scale=0.1, bias=sb["sinb"][:, ft:ft + 1]),
                       waits=[tokh, tok_sinb, ring["s1"][s1sl]])
            sin_toks[ft] = a_sin
            d_q = OP("vector", lambda e, psh=psh, ft=ft, s1sl=s1sl:
                     e.scalar_tensor_tensor(out=sb["q"][:, s1sl * T:(s1sl + 1) * T],
                                            in0=sb["s1"][:, s1sl * T:(s1sl + 1) * T],
                                            scalar=sb["V"][:, ft:ft + 1], in1=psh[:, :],
                                            op0=ALU.mult, op1=ALU.add),
                     waits=[a_sin, tok_V, ring["q"][s1sl]])
            q_toks[ft] = d_q
            ring["s1"][s1sl] = d_q
            d_act = OP("vector", lambda e, ft=ft, s1sl=s1sl:
                       e.tensor_scalar(out=sb["act"][:, ft * T:(ft + 1) * T],
                                       in0=sb["q"][:, s1sl * T:(s1sl + 1) * T],
                                       scalar1=sb["vb1"][:, ft:ft + 1], scalar2=0.0,
                                       op0=ALU.add, op1=ALU.max),
                       waits=[d_q, prev["ffn2B"].get(ft)])
            ring["q"][s1sl] = d_act
            tok_act[ft] = d_act
        prev["pshfree"] = [[sin_toks[FT - 2], q_toks[FT - 2]],
                           [sin_toks[FT - 1], q_toks[FT - 1]]]
        prev["ffn1_last_pe"] = w1_cons[i * FT + FT - 1]

        for dc in range(DC):
            ring["xn"][dc] = prev["ffn1_last_pe"]

        new_ffn2B = {}
        nxt_xn = [None] * DC
        for pas in (range(2) if SUB & 256 else []):
            pef = None
            for fc in range(FT):
                g2 = i * 2 * FT + pas * FT + fc
                sl = g2 % 3
                w2_ready = (f"w2_{sl}", (g2 // 3 + 1) * 16)
                w0 = [w2_ready, tok_act[fc]]
                if fc == 0:
                    w0 = w0 + [prev["silu_bank"][j] for j in range(4)]
                lastmm = None
                for j in range(4):
                    dt = pas * 4 + j
                    inc = (j == 3)
                    lastmm = OP("tensor", lambda e, j=j, sl=sl, dt=dt, fc=fc:
                                e.matmul(out=PS_F[j][:, :],
                                         lhsT=sb["w2b"][:, sl * 1024 + dt * 128: sl * 1024 + (dt + 1) * 128],
                                         rhs=sb["act"][:, fc * T:(fc + 1) * T],
                                         start=(fc == 0), stop=(fc == FT - 1)),
                                waits=(w0 if j == 0 else ()), inc=inc)
                w2_cons[g2] = lastmm
                emit_w2_load(g2 + 3)
                if pas == 1:
                    new_ffn2B[fc] = lastmm
                if fc == FT - 1:
                    pef = lastmm
            for j in range(4):
                dt = pas * 4 + j
                gsl = j % 2
                a_g = OP("scalar", lambda e, dt=dt, gsl=gsl, j=j:
                         e.activation(out=sb["g"][:, gsl * T:(gsl + 1) * T], in_=PS_F[j][:, :],
                                      func=AF.Silu, bias=sb["vb2"][:, dt:dt + 1]),
                         waits=[pef, ring["g"][gsl]])
                prev["silu_bank"][j] = a_g
                qsl = j % 2
                d_p = OP("vector", lambda e, dt=dt, gsl=gsl, qsl=qsl:
                         e.tensor_mul(out=sb["q"][:, qsl * T:(qsl + 1) * T],
                                      in0=sb["g"][:, gsl * T:(gsl + 1) * T],
                                      in1=sb["s"][:, dt * T:(dt + 1) * T]),
                         waits=[a_g, prev["scan"][dt], ring["q"][qsl]])
                ring["g"][gsl] = d_p
                osl = dt % 2
                ou_occ[osl] += 1
                xt_ap = sb["xt"][:, dt * T:(dt + 1) * T]
                d_o = OP("vector", lambda e, qsl=qsl, osl=osl, x2=xt_ap:
                         e.tensor_add(out=sb["outb"][:, osl * T:(osl + 1) * T],
                                      in0=sb["q"][:, qsl * T:(qsl + 1) * T], in1=x2),
                         waits=[d_p, (f"ou{osl}", (ou_occ[osl] - 1) * 16),
                                ring["xt_ready"][dt]])
                ring["q"][qsl] = d_o
                prev["combine"][dt] = d_o
                dst = d_out[b, dt * 128:(dt + 1) * 128, t0:t0 + T]
                DMA("sync", lambda e, d2=dst, osl=osl:
                    e.dma_start(out=d2, in_=sb["outb"][:, osl * T:(osl + 1) * T]),
                    f"ou{osl}", waits=[d_o])
                if i + 1 < NCH:
                    emit_xt_slice(i + 1, dt, d_o)
                    nxt_xn[dt] = emit_xn_dc(i + 1, dt, bcn)
        prev["ffn2B"] = new_ffn2B
        if (SUB & 256) and i + 1 < NCH:
            prev["xn"] = [(o2, o3) for (o1, o2, o3) in nxt_xn]
            ring["psrt"][0] = nxt_xn[DC - 1][0]
            ring["psit"][0] = nxt_xn[DC - 1][1]

    for slx in range(2):
        W("sync", (f"ou{slx}", dcnt[f"ou{slx}"]))

    with stack:
        with nc.Block() as block:
            @block.sync
            def _(e):
                for f in prog["sync"]:
                    f(e)

            @block.tensor
            def _(e):
                for f in prog["tensor"]:
                    f(e)

            @block.vector
            def _(e):
                for f in prog["vector"]:
                    f(e)

            @block.scalar
            def _(e):
                for f in prog["scalar"]:
                    f(e)

            @block.gpsimd
            def _(e):
                for f in prog["gpsimd"]:
                    f(e)

    return nc


_NC_CACHE = None


def kernel(**inputs):
    global _NC_CACHE
    x = np.asarray(inputs["L0_attn_x"], dtype=np.float32)
    wa = np.asarray(inputs["L0_attn_Wa"], dtype=np.float32)
    wx = np.asarray(inputs["L0_attn_Wx"], dtype=np.float32)
    ba = np.asarray(inputs["L0_attn_ba"], dtype=np.float32)
    bx = np.asarray(inputs["L0_attn_bx"], dtype=np.float32)
    lam = np.asarray(inputs["L0_attn_Lam"], dtype=np.float32)
    w1 = np.asarray(inputs["L0_ffn_W1"], dtype=np.float32)
    b1 = np.asarray(inputs["L0_ffn_b1"], dtype=np.float32)
    th = np.asarray(inputs["L0_ffn_quantum_phases"], dtype=np.float32)
    wr = np.asarray(inputs["L0_ffn_quantum_weights_real"], dtype=np.float32)
    wi = np.asarray(inputs["L0_ffn_quantum_weights_imag"], dtype=np.float32)
    w2 = np.asarray(inputs["L0_ffn_W2"], dtype=np.float32)
    b2 = np.asarray(inputs["L0_ffn_b2"], dtype=np.float32)

    def vpack(v, k):
        return np.ascontiguousarray(v.reshape(k, 128).T)

    common = {
        "waT": np.ascontiguousarray(wa.T), "wxT": np.ascontiguousarray(wx.T),
        "w1": np.ascontiguousarray(w1), "w2": np.ascontiguousarray(w2),
        "vba": vpack(ba, DC), "vbx": vpack(bx, DC), "vb2": vpack(b2, DC),
        "vlam": vpack(lam, DC), "vb1": vpack(b1, FT), "vth": vpack(th, FT),
        "vwr": vpack(wr, FT), "vwi": vpack(wi, FT),
        "iden": np.eye(128, dtype=np.float32),
    }
    in_maps = []
    for c in range(NCORE):
        xc = x[c * BC:(c + 1) * BC]
        m = dict(common)
        m["x"] = np.ascontiguousarray(xc)
        m["xT"] = np.ascontiguousarray(xc.transpose(0, 2, 1))
        in_maps.append(m)

    if _NC_CACHE is None:
        _NC_CACHE = build_program()
    res = run_bass_kernel_spmd(_NC_CACHE, in_maps, core_ids=list(range(NCORE)))
    out = np.empty((B, S, D), dtype=np.float32)
    for c in range(NCORE):
        out[c * BC:(c + 1) * BC] = np.asarray(
            res.results[c]["outT"], dtype=np.float32).transpose(0, 2, 1)
    return out


if __name__ == "__main__":
    rng = np.random.default_rng(0)
    ins = {
        "L0_attn_x": rng.standard_normal((B, S, D)).astype(np.float32),
        "L0_attn_Wa": (rng.standard_normal((D, D)) / 32).astype(np.float32),
        "L0_attn_Wx": (rng.standard_normal((D, D)) / 32).astype(np.float32),
        "L0_attn_ba": (rng.standard_normal(D) * 0.02).astype(np.float32),
        "L0_attn_bx": (rng.standard_normal(D) * 0.02).astype(np.float32),
        "L0_attn_Lam": (4.0 + 5 * rng.random(D)).astype(np.float32),
        "L0_ffn_W1": (rng.standard_normal((D, FF)) / 32).astype(np.float32),
        "L0_ffn_b1": (rng.standard_normal(FF) * 0.2).astype(np.float32),
        "L0_ffn_quantum_phases": (rng.standard_normal(FF) * 6.28).astype(np.float32),
        "L0_ffn_quantum_weights_real": (rng.standard_normal(FF) * 0.1).astype(np.float32),
        "L0_ffn_quantum_weights_imag": (rng.standard_normal(FF) * 0.1).astype(np.float32),
        "L0_ffn_W2": (rng.standard_normal((FF, D)) / 64).astype(np.float32),
        "L0_ffn_b2": (rng.standard_normal(D) * 0.2).astype(np.float32),
    }
    out = kernel(**ins)
    print("ran", out.shape, out.dtype, float(np.abs(out).mean()))


# version 03
# speedup vs baseline: 1.0115x; 1.0115x over previous
import numpy as np
import concourse.bass as bass
import concourse.mybir as mybir
from concourse.bass_utils import run_bass_kernel_spmd

F32 = mybir.dt.float32
BF16 = mybir.dt.bfloat16
AF = mybir.ActivationFunctionType
ALU = mybir.AluOpType
AX = mybir.AxisListType

import os
B, S, D, FF = 16, 4096, 1024, 4096
NCORE = 8
BC = B // NCORE
T = 512
NCH = BC * S // T
NCH = int(os.environ.get("KERNEL_NCH", NCH))
PHASE = int(os.environ.get("KERNEL_PHASE", 99))
SUB = int(os.environ.get("KERNEL_SUB", 1023))
DC = D // 128
FT = FF // 128
TT = T // 128
EPS = 1e-5

PI = float(np.float32(np.pi))
TWO_PI_HI = float(np.float32(2 * np.pi))
TWO_PI_LO = float(np.float32(2 * np.pi - np.float64(np.float32(2 * np.pi))))
PI_HI = float(np.float32(np.pi))
PI_LO = float(np.float32(np.pi - np.float64(np.float32(np.pi))))
HPI = float(np.float32(np.pi / 2))
HPI_HI = HPI
HPI_LO = float(np.float32(np.pi / 2 - np.float64(np.float32(np.pi / 2))))

ENGSEM = {"tensor": "pe", "vector": "dv", "scalar": "ac"}


def build_program():
    nc = bass.Bass("TRN2", target_bir_lowering=False, detect_race_conditions=False)

    d_x = nc.dram_tensor("x", [BC, S, D], F32, kind="ExternalInput").ap()
    d_xT = nc.dram_tensor("xT", [BC, D, S], F32, kind="ExternalInput").ap()
    d_waT = nc.dram_tensor("waT", [D, D], F32, kind="ExternalInput").ap()
    d_wxT = nc.dram_tensor("wxT", [D, D], F32, kind="ExternalInput").ap()
    d_w1 = nc.dram_tensor("w1", [D, FF], F32, kind="ExternalInput").ap()
    d_w2 = nc.dram_tensor("w2", [FF, D], F32, kind="ExternalInput").ap()
    d_vecs = {}
    for nme in ["vba", "vbx", "vb2", "vlam"]:
        d_vecs[nme] = nc.dram_tensor(nme, [128, DC], F32, kind="ExternalInput").ap()
    for nme in ["vb1", "vth", "vwr", "vwi"]:
        d_vecs[nme] = nc.dram_tensor(nme, [128, FT], F32, kind="ExternalInput").ap()
    d_vecs["iden"] = nc.dram_tensor("iden", [128, 128], F32, kind="ExternalInput").ap()
    d_out = nc.dram_tensor("outT", [BC, D, S], F32, kind="ExternalOutput").ap()
    d_w1s = nc.dram_tensor("w1s", [FT, D, 128], BF16, kind="Internal").ap()
    d_w2s = nc.dram_tensor("w2s", [FF, D], BF16, kind="Internal").ap()

    import contextlib
    stack = contextlib.ExitStack()
    sb = {}

    def S_(name, shape, dtype=F32):
        sb[name] = stack.enter_context(nc.sbuf_tensor("sb_" + name, list(shape), dtype))

    S_("wa", (128, DC * 1024), BF16)
    S_("wx", (128, DC * 1024), BF16)
    S_("w1b", (128, 4 * 1024), BF16)
    S_("w2b", (128, 4 * 1024), BF16)
    S_("xt", (128, DC * T))
    S_("xk", (128, 2 * 2048))
    S_("xn32", (128, DC * T))
    S_("xnbf", (128, DC * T), BF16)
    S_("s", (128, DC * T))
    S_("act", (128, FT * T), BF16)
    S_("it", (128, 4 * T))
    S_("a", (128, 4 * T))
    S_("asq", (128, 4 * T))
    S_("sq", (128, 4 * T))
    S_("s1", (128, 2 * T))
    S_("q", (128, 2 * T))
    S_("g", (128, T))
    S_("outb", (128, 2 * T))
    S_("sqs", (128, 512))
    S_("strow", (1, 1024))
    S_("ssq8", (128, 2 * TT))
    S_("stats8", (128, 8 * TT))
    for k, nme in enumerate(["ssum", "ssq", "mean", "var", "var2", "std",
                             "rstd", "mr"]):
        sb[nme] = sb["stats8"].ap()[:, k * TT:(k + 1) * TT]
    for nme in ["vba", "vbx", "vb2", "vlam", "coefh", "ba2", "bx2"]:
        S_(nme, (128, DC))
    for nme in ["vb1", "vth", "vwr", "vwi", "phi", "V", "sinb"]:
        S_(nme, (128, FT))
    S_("iden", (128, 128))
    S_("ones", (1, 128))
    S_("carry", (128, DC))

    act_f32 = sb["act"].ap().bitcast(F32)

    def stage_ap(sl):
        return act_f32[:, 4096 + sl * 1024: 4096 + (sl + 1) * 1024]

    ps = [stack.enter_context(nc.psum_tensor(f"ps{i}", [128, T], F32))
          for i in range(8)]
    PS_RT, PS_IT, PS_H0, PS_H1 = ps[0], ps[1], ps[2], ps[3]
    PS_F = ps[4:8]

    semnames = (["pe", "dv", "ac", "mis"] + [f"scr{k}" for k in range(4)]
                + ["xt"] + [f"xk{k}" for k in range(2)]
                + [f"w1_{k}" for k in range(4)] + [f"w2_{k}" for k in range(4)]
                + [f"st{k}" for k in range(4)] + [f"ou{k}" for k in range(2)])
    sems = {n: stack.enter_context(nc.semaphore(f"s_{n}")) for n in semnames}

    prog = {e: [] for e in ["sync", "tensor", "vector", "scalar", "gpsimd"]}
    cnt = {"pe": 0, "dv": 0, "ac": 0}
    dcnt = {n: 0 for n in semnames if n not in cnt}
    last_wait = {e: {} for e in prog}

    def W(eng, tok):
        if tok is None:
            return
        if isinstance(tok, (list, tuple)) and tok and isinstance(tok[0], (list, tuple, type(None))):
            for t in tok:
                W(eng, t)
            return
        key, val = tok
        if val <= 0:
            return
        lw = last_wait[eng]
        if lw.get(key, 0) >= val:
            return
        lw[key] = val
        sem = sems[key]
        prog[eng].append(lambda e, sem=sem, val=val: e.wait_ge(sem, val))

    def OP(eng, fn, waits=(), inc=True):
        for t in waits:
            W(eng, t)
        if inc:
            key = ENGSEM[eng]
            cnt[key] += 1
            tok = (key, cnt[key])
            sem = sems[key]
            prog[eng].append(lambda e, fn=fn, sem=sem: fn(e).then_inc(sem, 1))
            return tok
        prog[eng].append(lambda e, fn=fn: fn(e))
        return None

    def DMA(eng, fn, semkey, waits=()):
        for t in waits:
            W(eng, t)
        dcnt[semkey] += 16
        tok = (semkey, dcnt[semkey])
        sem = sems[semkey]
        prog[eng].append(lambda e, fn=fn, sem=sem: fn(e).then_inc(sem, 16))
        return tok

    ring = {
        "xkslot": [None, None],
        "it": [None] * 4, "a": [None] * 4, "asq": [None] * 4,
        "sq": [None] * 4, "s1": [None, None],
        "q": [None, None], "g": [None],
        "xt_ready": [None] * DC, "xk_ready": [None, None],
        "psrt": [None], "psit": [None], "bc": [None],
        "xn": [None] * DC,
    }
    w1_cons = {}
    w2_cons = {}
    ou_occ = [0, 0]
    state = {"w1_g": 0, "w2_g": 0}

    PRELOAD_FIRST = True
    mis_toks = []
    for nme in ["iden", "vba", "vbx", "vb2", "vlam", "vb1", "vth", "vwr", "vwi"]:
        src = d_vecs[nme]
        mis_toks.append(DMA("sync", lambda e, s2=src, t2=sb[nme]:
                            e.dma_start(out=t2[:, :], in_=s2), "mis"))
    tok_mis = ("mis", dcnt["mis"])

    if SUB & 8:
        for dc in range(DC):
            src0 = d_xT[0, dc * 128:(dc + 1) * 128, 0:T]
            dst0 = sb["xt"][:, dc * T:(dc + 1) * T]
            ring["xt_ready"][dc] = DMA(
                "gpsimd", lambda e, s2=src0, d2=dst0: e.dma_start(out=d2, in_=s2),
                "xt")
        for h in range(2):
            src0 = d_x[0, h * 256:(h + 1) * 256, :]
            src0 = src0.rearrange("(tt p) d2 -> p tt d2", p=128)
            dst0 = sb["xk"][:, h * 2048:(h + 1) * 2048]
            dst0 = dst0.rearrange("p (tt d2) -> p tt d2", d2=D)
            ring["xk_ready"][h] = DMA(
                "sync", lambda e, s2=src0, d2=dst0: e.dma_start(out=d2, in_=s2),
                f"xk{h}")

    stage_jobs = []
    if PHASE < 2:
        global_stage_skip = True
    for dc in range(DC):
        stage_jobs.append(("wa", d_waT[dc * 128:(dc + 1) * 128, :], dc))
    for dc in range(DC):
        stage_jobs.append(("wx", d_wxT[dc * 128:(dc + 1) * 128, :], dc))
    for dc in range(DC):
        for qq in range(4):
            stage_jobs.append(("w1", d_w1[dc * 128:(dc + 1) * 128,
                                          qq * 1024:(qq + 1) * 1024], (dc, qq)))
    for fc in range(FT):
        stage_jobs.append(("w2", d_w2[fc * 128:(fc + 1) * 128, :], fc))

    stg_cons = [None] * 4
    bfw1_cons = [None] * 4
    bfw2_cons = [None] * 4
    w1_wr_last = [None] * 4
    w2_wr_last = [None] * 4
    tok_wa_done = None
    if PHASE < 2:
        stage_jobs = []
    for idx, (kind, src, meta) in enumerate(stage_jobs):
        sl = idx % 4
        ld = DMA("sync", lambda e, s2=src, sl=sl:
                 e.dma_start(out=stage_ap(sl), in_=s2),
                 f"st{sl}", waits=[stg_cons[sl]])
        if kind in ("wa", "wx"):
            dst, dc = sb[kind], meta
            cv = OP("vector", lambda e, dst=dst, dc=dc, sl=sl:
                    e.tensor_copy(out=dst[:, dc * 1024:(dc + 1) * 1024],
                                  in_=stage_ap(sl)),
                    waits=[ld])
            stg_cons[sl] = cv
            if kind == "wx" and meta == DC - 1:
                tok_wa_done = cv
        else:
            bfbuf = sb["w1b"] if kind == "w1" else sb["w2b"]
            bcons = bfw1_cons if kind == "w1" else bfw2_cons
            cv = OP("vector", lambda e, sl=sl, bfbuf=bfbuf:
                    e.tensor_copy(out=bfbuf[:, sl * 1024:(sl + 1) * 1024],
                                  in_=stage_ap(sl)),
                    waits=[ld, bcons[sl]])
            stg_cons[sl] = cv
            if PHASE < 3:
                bcons[sl] = None
                continue
            sem = f"scr{sl}"
            if kind == "w1":
                dc, qq = meta
                dstap = d_w1s[qq * 8:(qq + 1) * 8, dc * 128:(dc + 1) * 128, :]
                dstap = dstap.rearrange("ft p f -> p ft f")
                srcap = sb["w1b"][:, sl * 1024:(sl + 1) * 1024]
                srcap = srcap.rearrange("p (ft f) -> p ft f", f=128)
                wr = DMA("sync", lambda e, d2=dstap, s2=srcap:
                         e.dma_start(out=d2, in_=s2), sem, waits=[cv])
                w1_wr_last[sl] = wr
            else:
                fc = meta
                dstap = d_w2s[fc * 128:(fc + 1) * 128, :]
                srcap = sb["w2b"][:, sl * 1024:(sl + 1) * 1024]
                wr = DMA("sync", lambda e, d2=dstap, s2=srcap:
                         e.dma_start(out=d2, in_=s2), sem, waits=[cv])
                w2_wr_last[sl] = wr
            bcons[sl] = wr
    tok_w1s = [t for t in w1_wr_last if t]
    tok_w2s = [t for t in w2_wr_last if t]

    u1 = act_f32[:, 3968:4000]
    u2 = act_f32[:, 4000:4032]
    u3 = act_f32[:, 4032:4064]
    u4 = sb["sinb"]
    if SUB & 1:
        t_ones = OP("vector", lambda e: e.memset(sb["ones"][:, :], 1.0))
        t1 = OP("vector", lambda e: e.tensor_scalar_mul(out=u1[:, :DC], in0=sb["vlam"][:, :], scalar1=-1.0), waits=[tok_mis])
        t2 = OP("vector", lambda e: e.tensor_scalar_max(out=u2[:, :DC], in0=u1[:, :DC], scalar1=0.0), waits=[t1])
        t3 = OP("scalar", lambda e: e.activation(out=u3[:, :DC], in_=sb["vlam"][:, :], func=AF.Abs), waits=[tok_mis])
        a1 = OP("scalar", lambda e: e.activation(out=u4[:, :DC], in_=u3[:, :DC], func=AF.Exp, scale=-1.0), waits=[t3])
        a2 = OP("scalar", lambda e: e.activation(out=u3[:, :DC], in_=u4[:, :DC], func=AF.Ln, bias=1.0), waits=[a1])
        t4 = OP("vector", lambda e: e.tensor_add(out=u1[:, :DC], in0=u2[:, :DC], in1=u3[:, :DC]), waits=[t2, a2])
        tok_coef = OP("vector", lambda e: e.tensor_scalar_mul(out=sb["coefh"][:, :], in0=u1[:, :DC], scalar1=-4.0), waits=[t4])
        tok_ba2 = OP("vector", lambda e: e.tensor_scalar_mul(out=sb["ba2"][:, :], in0=sb["vba"][:, :], scalar1=0.5), waits=[tok_mis])
        tok_bx2 = OP("vector", lambda e: e.tensor_scalar_mul(out=sb["bx2"][:, :], in0=sb["vbx"][:, :], scalar1=0.5), waits=[tok_mis])

        pt = OP("vector", lambda e: e.tensor_copy(out=sb["phi"][:, :], in_=sb["vth"][:, :]), waits=[tok_mis])
        for _ in range(6):
            m1 = OP("vector", lambda e: e.tensor_single_scalar(out=u1[:, :], in_=sb["phi"][:, :], scalar=PI, op=ALU.is_gt), waits=[pt])
            p1 = OP("vector", lambda e: e.scalar_tensor_tensor(out=sb["phi"][:, :], in0=u1[:, :], scalar=-TWO_PI_HI, in1=sb["phi"][:, :], op0=ALU.mult, op1=ALU.add), waits=[m1])
            p2 = OP("vector", lambda e: e.scalar_tensor_tensor(out=sb["phi"][:, :], in0=u1[:, :], scalar=-TWO_PI_LO, in1=sb["phi"][:, :], op0=ALU.mult, op1=ALU.add), waits=[p1])
            m2 = OP("vector", lambda e: e.tensor_single_scalar(out=u1[:, :], in_=sb["phi"][:, :], scalar=-PI, op=ALU.is_lt), waits=[p2])
            p3 = OP("vector", lambda e: e.scalar_tensor_tensor(out=sb["phi"][:, :], in0=u1[:, :], scalar=TWO_PI_HI, in1=sb["phi"][:, :], op0=ALU.mult, op1=ALU.add), waits=[m2])
            pt = OP("vector", lambda e: e.scalar_tensor_tensor(out=sb["phi"][:, :], in0=u1[:, :], scalar=TWO_PI_LO, in1=sb["phi"][:, :], op0=ALU.mult, op1=ALU.add), waits=[p3])
        mh = OP("vector", lambda e: e.tensor_single_scalar(out=u1[:, :], in_=sb["phi"][:, :], scalar=HPI, op=ALU.is_gt), waits=[pt])
        ml = OP("vector", lambda e: e.tensor_single_scalar(out=u2[:, :], in_=sb["phi"][:, :], scalar=-HPI, op=ALU.is_lt), waits=[mh])
        r1 = OP("vector", lambda e: e.scalar_tensor_tensor(out=sb["phi"][:, :], in0=u1[:, :], scalar=-PI_HI, in1=sb["phi"][:, :], op0=ALU.mult, op1=ALU.add), waits=[ml])
        r2 = OP("vector", lambda e: e.scalar_tensor_tensor(out=sb["phi"][:, :], in0=u1[:, :], scalar=-PI_LO, in1=sb["phi"][:, :], op0=ALU.mult, op1=ALU.add), waits=[r1])
        r3 = OP("vector", lambda e: e.scalar_tensor_tensor(out=sb["phi"][:, :], in0=u2[:, :], scalar=PI_HI, in1=sb["phi"][:, :], op0=ALU.mult, op1=ALU.add), waits=[r2])
        tok_phi = OP("vector", lambda e: e.scalar_tensor_tensor(out=sb["phi"][:, :], in0=u2[:, :], scalar=PI_LO, in1=sb["phi"][:, :], op0=ALU.mult, op1=ALU.add), waits=[r3])
        s_sin = OP("scalar", lambda e: e.activation(out=u1[:, :], in_=sb["phi"][:, :], func=AF.Sin), waits=[tok_phi])
        c1 = OP("vector", lambda e: e.tensor_scalar_add(out=u2[:, :], in0=sb["phi"][:, :], scalar1=HPI_HI), waits=[tok_phi])
        c2 = OP("vector", lambda e: e.tensor_scalar_add(out=u2[:, :], in0=u2[:, :], scalar1=HPI_LO), waits=[c1])
        s_cos = OP("scalar", lambda e: e.activation(out=u3[:, :], in_=u2[:, :], func=AF.Sin), waits=[c2])
        v1 = OP("vector", lambda e: e.tensor_mul(out=u2[:, :], in0=u3[:, :], in1=sb["vwr"][:, :]), waits=[s_cos])
        v2 = OP("vector", lambda e: e.tensor_mul(out=u4[:, :], in0=u1[:, :], in1=sb["vwi"][:, :]), waits=[s_sin, v1])
        v3 = OP("vector", lambda e: e.tensor_add(out=u2[:, :], in0=u2[:, :], in1=u4[:, :]), waits=[v2])
        v4 = OP("vector", lambda e: e.tensor_mul(out=u2[:, :], in0=u2[:, :], in1=sb["vwi"][:, :]), waits=[v3])
        tok_V = OP("vector", lambda e: e.tensor_scalar_mul(out=sb["V"][:, :], in0=u2[:, :], scalar1=0.1), waits=[v4])
        b11 = OP("vector", lambda e: e.tensor_scalar_mul(out=u4[:, :], in0=sb["vb1"][:, :], scalar1=0.1), waits=[v2])
        tok_sinb = OP("vector", lambda e: e.tensor_add(out=sb["sinb"][:, :], in0=sb["phi"][:, :], in1=u4[:, :]), waits=[b11])


    else:
        t_ones = tok_coef = tok_ba2 = tok_bx2 = tok_V = tok_sinb = None

    def chunk_bt(i):
        return i // (S // T), (i % (S // T)) * T

    def emit_xt_slice(i, dc, free_tok):
        b, t0 = chunk_bt(i)
        src = d_xT[b, dc * 128:(dc + 1) * 128, t0:t0 + T]
        dst = sb["xt"][:, dc * T:(dc + 1) * T]
        ring["xt_ready"][dc] = DMA(
            "gpsimd", lambda e, s2=src, d2=dst: e.dma_start(out=d2, in_=s2),
            "xt", waits=[free_tok])

    def emit_xk_loads(i):
        b, t0 = chunk_bt(i)
        for h in range(2):
            src = d_x[b, t0 + h * 256:t0 + (h + 1) * 256, :]
            src = src.rearrange("(tt p) d2 -> p tt d2", p=128)
            dst = sb["xk"][:, h * 2048:(h + 1) * 2048]
            dst = dst.rearrange("p (tt d2) -> p tt d2", d2=D)
            ring["xk_ready"][h] = DMA(
                "sync", lambda e, s2=src, d2=dst: e.dma_start(out=d2, in_=s2),
                f"xk{h}", waits=[ring["xkslot"][h]])

    def emit_w1_load(g):
        if g >= NCH * FT:
            return
        ft = g % FT
        sl = g % 4
        src = d_w1s[ft].rearrange("(dc p) f -> p dc f", p=128)
        dst = sb["w1b"][:, sl * 1024:(sl + 1) * 1024]
        dst = dst.rearrange("p (dc f) -> p dc f", f=128)
        waits = [w1_cons.get(g - 4)] if g >= 4 else list(tok_w1s)
        DMA("sync", lambda e, s2=src, d2=dst: e.dma_start(out=d2, in_=s2),
            f"w1_{sl}", waits=waits)

    def emit_w2_load(g):
        if g >= NCH * 2 * FT:
            return
        fc = g % FT
        sl = g % 4
        src = d_w2s[fc * 128:(fc + 1) * 128, :]
        dst = sb["w2b"][:, sl * 1024:(sl + 1) * 1024]
        waits = [w2_cons.get(g - 4)] if g >= 4 else list(tok_w2s)
        DMA("sync", lambda e, s2=src, d2=dst: e.dma_start(out=d2, in_=s2),
            f"w2_{sl}", waits=waits)

    def emit_stats_pre(i):
        sq_toks = []
        rd_toks = []
        for tt in range(TT):
            sl = tt // 2
            col = (tt % 2) * 1024
            for hh in range(2):
                xkap = sb["xk"][:, sl * 2048 + col + hh * 512:
                               sl * 2048 + col + (hh + 1) * 512]
                sq_toks.append(OP(
                    "scalar", lambda e, x2=xkap, tt=tt, hh=hh:
                    e.activation(out=sb["sqs"][:, :], in_=x2, func=AF.Square,
                                 accum_out=sb["ssq8"][:, hh * TT + tt:hh * TT + tt + 1]),
                    waits=[ring["xk_ready"][sl]]))
            xkap = sb["xk"][:, sl * 2048 + col: sl * 2048 + col + 1024]
            rd_toks.append(OP(
                "vector", lambda e, x2=xkap, tt=tt:
                e.reduce_sum(out=sb["ssum"][:, tt:tt + 1], in_=x2, axis=AX.X),
                waits=[ring["xk_ready"][sl]]))
        ring["xkslot"][0] = [rd_toks[1], sq_toks[3]]
        ring["xkslot"][1] = [rd_toks[3], sq_toks[7]]
        mc = OP("vector", lambda e: e.tensor_add(
            out=sb["ssq"][:, :], in0=sb["ssq8"][:, 0:TT], in1=sb["ssq8"][:, TT:2 * TT]),
            waits=[sq_toks[-1]])
        m1 = OP("vector", lambda e: e.tensor_scalar_mul(out=sb["mean"][:, :], in0=sb["ssum"][:, :], scalar1=1.0 / D),
                waits=[rd_toks[-1]])
        m2 = OP("vector", lambda e: e.tensor_mul(out=sb["var"][:, :], in0=sb["mean"][:, :], in1=sb["mean"][:, :]),
                waits=[m1])
        m3 = OP("vector", lambda e: e.scalar_tensor_tensor(out=sb["var2"][:, :], in0=sb["ssq"][:, :], scalar=1.0 / D, in1=sb["var"][:, :], op0=ALU.mult, op1=ALU.subtract),
                waits=[m2, mc])
        m4 = OP("vector", lambda e: e.tensor_scalar_add(out=sb["var2"][:, :], in0=sb["var2"][:, :], scalar1=EPS),
                waits=[m3])
        return {"var": m4}

    def emit_stats_post_a(toks):
        a_std = OP("scalar", lambda e: e.activation(out=sb["std"][:, :], in_=sb["var2"][:, :], func=AF.Sqrt),
                   waits=[toks["var"]])
        d_rs = OP("vector", lambda e: e.reciprocal(out=sb["rstd"][:, :], in_=sb["std"][:, :]),
                  waits=[a_std])
        d_mr = OP("vector", lambda e: e.tensor_mul(out=sb["mr"][:, :], in0=sb["mean"][:, :], in1=sb["rstd"][:, :]),
                  waits=[d_rs])
        return {"rs": d_rs, "mr": d_mr}

    def emit_stats_post_b(pt):
        d_rs, d_mr = pt["rs"], pt["mr"]
        tr_toks = []
        for tt in range(TT):
            tr_toks.append(OP(
                "tensor", lambda e, tt=tt:
                e.transpose(out=PS_RT[0:1, tt * 128:(tt + 1) * 128],
                            in_=sb["rstd"][:, tt:tt + 1], identity=sb["iden"][:, :]),
                waits=[d_rs, ring["psrt"][0]]))
        for tt in range(TT):
            tr_toks.append(OP(
                "tensor", lambda e, tt=tt:
                e.transpose(out=PS_IT[0:1, tt * 128:(tt + 1) * 128],
                            in_=sb["mr"][:, tt:tt + 1], identity=sb["iden"][:, :]),
                waits=[d_mr, ring["psit"][0]]))
        srow_toks = []
        for tt in range(TT):
            srow_toks.append(OP(
                "vector", lambda e, tt=tt:
                e.tensor_copy(out=sb["strow"][0:1, tt * 128:(tt + 1) * 128],
                              in_=PS_RT[0:1, tt * 128:(tt + 1) * 128]),
                waits=[tr_toks[TT - 1]]))
        for tt in range(TT):
            srow_toks.append(OP(
                "vector", lambda e, tt=tt:
                e.tensor_copy(out=sb["strow"][0:1, 512 + tt * 128:512 + (tt + 1) * 128],
                              in_=PS_IT[0:1, tt * 128:(tt + 1) * 128]),
                waits=[tr_toks[2 * TT - 1]]))
        bc1 = OP("tensor", lambda e: e.matmul(out=PS_RT[:, :], lhsT=sb["ones"][:, :], rhs=sb["strow"][0:1, 0:512], start=True, stop=True),
                 waits=srow_toks[0:4] + [t_ones])
        bc2 = OP("tensor", lambda e: e.matmul(out=PS_IT[:, :], lhsT=sb["ones"][:, :], rhs=sb["strow"][0:1, 512:1024], start=True, stop=True),
                 waits=srow_toks[4:8])
        return bc1, bc2

    def emit_stats_post(i, toks):
        return emit_stats_post_b(emit_stats_post_a(toks))

    prev = {
        "xn": None, "combine": [None] * DC, "scan": [None] * DC,
        "carry": [None] * DC, "silu_bank": [None] * 4,
        "ffn1_last_pe": None, "ffn2B": {}, "pshfree": [None, None],
    }

    def emit_xn_dc(i, dc, bc_toks):
        bc1, bc2 = bc_toks
        xt_ap = sb["xt"][:, dc * T:(dc + 1) * T]
        w0 = [ring["xt_ready"][dc], bc1, bc2, ring["xn"][dc], prev["scan"][dc]]
        o1 = OP("vector", lambda e, x2=xt_ap, dc=dc:
                e.tensor_mul(out=sb["xn32"][:, dc * T:(dc + 1) * T], in0=x2, in1=PS_RT[:, :]),
                waits=w0)
        o2 = OP("vector", lambda e, dc=dc:
                e.tensor_sub(out=sb["xn32"][:, dc * T:(dc + 1) * T],
                             in0=sb["xn32"][:, dc * T:(dc + 1) * T], in1=PS_IT[:, :]),
                waits=[o1])
        o3 = OP("vector", lambda e, dc=dc:
                e.tensor_copy(out=sb["xnbf"][:, dc * T:(dc + 1) * T],
                              in_=sb["xn32"][:, dc * T:(dc + 1) * T]),
                waits=[o2])
        return (o1, o2, o3)

    if SUB & 4:
        for g in range(4):
            emit_w1_load(g)
            emit_w2_load(g)
    if SUB & 2:
        st0 = emit_stats_pre(0)
    if SUB & 16:
        bc0 = emit_stats_post(0, st0)
    if SUB & 32:
        _xn0 = [emit_xn_dc(0, dc, bc0) for dc in range(DC)]
        prev["xn"] = [(o2, o3) for (o1, o2, o3) in _xn0]
        ring["psrt"][0] = _xn0[-1][0]
        ring["psit"][0] = _xn0[-1][1]

    for i in range(NCH):
        b, t0 = chunk_bt(i)
        batch_start = (i % (S // T)) == 0

        if i + 1 < NCH:
            emit_xk_loads(i + 1)

        tok_a = [None] * DC
        stn = None
        for half in (range(2) if SUB & 64 else []):
            dts = list(range(half * 4, half * 4 + 4))
            tok_it = {}
            tok_cl = {}
            for dt in dts:
                asl = dt % 4
                tokr = None
                for dc in range(DC):
                    last = dc == DC - 1
                    wdc = [prev["xn"][dc][1]]
                    if dc == 0:
                        wdc += [ring["psrt"][0], tok_wa_done]
                    tokr = OP("tensor", lambda e, dc=dc, dt=dt, last=last:
                              e.matmul(out=PS_RT[:, :],
                                       lhsT=sb["wa"][:, dc * 1024 + dt * 128: dc * 1024 + (dt + 1) * 128],
                                       rhs=sb["xnbf"][:, dc * T:(dc + 1) * T],
                                       start=(dc == 0), stop=last),
                              waits=wdc, inc=last)
                pe_rt = tokr
                toki = None
                for dc in range(DC):
                    last = dc == DC - 1
                    toki = OP("tensor", lambda e, dc=dc, dt=dt, last=last:
                              e.matmul(out=PS_IT[:, :],
                                       lhsT=sb["wx"][:, dc * 1024 + dt * 128: dc * 1024 + (dt + 1) * 128],
                                       rhs=sb["xnbf"][:, dc * T:(dc + 1) * T],
                                       start=(dc == 0), stop=last),
                              waits=([ring["psit"][0]] if dc == 0 else ()), inc=last)
                pe_it = toki
                a_th = OP("scalar", lambda e, dt=dt:
                          e.activation(out=sb["sqs"][:, :], in_=PS_RT[:, :], func=AF.Tanh,
                                       scale=0.5, bias=sb["ba2"][:, dt:dt + 1]),
                          waits=[pe_rt, tok_ba2, tok_coef])
                ring["psrt"][0] = a_th
                a_a = OP("scalar", lambda e, dt=dt, asl=asl:
                         e.activation(out=sb["a"][:, asl * T:(asl + 1) * T], in_=sb["sqs"][:, :],
                                      func=AF.Exp, scale=sb["coefh"][:, dt:dt + 1],
                                      bias=sb["coefh"][:, dt:dt + 1]),
                         waits=[a_th, ring["a"][asl]])
                tok_a[dt] = a_a
                a_ti = OP("scalar", lambda e, dt=dt, asl=asl:
                          e.activation(out=sb["it"][:, asl * T:(asl + 1) * T], in_=PS_IT[:, :],
                                       func=AF.Tanh, scale=0.5, bias=sb["bx2"][:, dt:dt + 1]),
                          waits=[pe_it, tok_bx2, ring["it"][asl]])
                ring["psit"][0] = a_ti
                a_sq = OP("scalar", lambda e, asl=asl:
                          e.activation(out=sb["asq"][:, asl * T:(asl + 1) * T],
                                       in_=sb["a"][:, asl * T:(asl + 1) * T], func=AF.Square),
                          waits=[a_a, ring["asq"][asl]])
                d_it = OP("vector", lambda e, asl=asl:
                          e.tensor_scalar(out=sb["it"][:, asl * T:(asl + 1) * T],
                                          in0=sb["it"][:, asl * T:(asl + 1) * T],
                                          scalar1=0.5, scalar2=0.5, op0=ALU.mult, op1=ALU.add),
                          waits=[a_ti])
                tok_it[dt] = d_it
                tok_cl[dt] = OP("vector", lambda e, asl=asl:
                                e.tensor_single_scalar(out=sb["asq"][:, asl * T:(asl + 1) * T],
                                                       in_=sb["asq"][:, asl * T:(asl + 1) * T],
                                                       scalar=1.0, op=ALU.min),
                                waits=[a_sq])

            if half == 0 and i + 1 < NCH:
                stn = emit_stats_pre(i + 1)

            tok_s = {}
            for dt in dts:
                asl = dt % 4
                tok_s[dt] = OP("scalar", lambda e, asl=asl:
                               e.activation(out=sb["sq"][:, asl * T:(asl + 1) * T],
                                            in_=sb["asq"][:, asl * T:(asl + 1) * T],
                                            func=AF.Sqrt, scale=-1.0, bias=1.0),
                               waits=[tok_cl[dt], ring["sq"][asl]])
            if half == 0 and i + 1 < NCH:
                stn_pt = emit_stats_post_a(stn)

            for dt in dts:
                asl = dt % 4
                d_t2 = OP("vector", lambda e, asl=asl, dt=dt:
                          e.tensor_mul(out=sb["asq"][:, asl * T:(asl + 1) * T],
                                       in0=sb["sq"][:, asl * T:(asl + 1) * T],
                                       in1=sb["xn32"][:, dt * T:(dt + 1) * T]),
                          waits=[tok_s[dt], prev["xn"][dt][0]])
                ring["sq"][asl] = d_t2
                d_b = OP("vector", lambda e, asl=asl:
                         e.tensor_mul(out=sb["asq"][:, asl * T:(asl + 1) * T],
                                      in0=sb["it"][:, asl * T:(asl + 1) * T],
                                      in1=sb["asq"][:, asl * T:(asl + 1) * T]),
                         waits=[d_t2, tok_it[dt]])
                ring["it"][asl] = d_b
                init = 0.0 if batch_start else sb["carry"][:, dt:dt + 1]
                d_sc = OP("vector", lambda e, asl=asl, dt=dt, init=init:
                          e.tensor_tensor_scan(out=sb["s"][:, dt * T:(dt + 1) * T],
                                               data0=sb["a"][:, asl * T:(asl + 1) * T],
                                               data1=sb["asq"][:, asl * T:(asl + 1) * T],
                                               initial=init, op0=ALU.mult, op1=ALU.add),
                          waits=[d_b, tok_a[dt], prev["combine"][dt], prev["carry"][dt]])
                ring["a"][asl] = d_sc
                ring["asq"][asl] = d_sc
                prev["scan"][dt] = d_sc
                prev["carry"][dt] = OP(
                    "vector", lambda e, dt=dt:
                    e.tensor_copy(out=sb["carry"][:, dt:dt + 1],
                                  in_=sb["s"][:, dt * T + T - 1: dt * T + T]),
                    waits=[d_sc])

        if (SUB & 64) and i + 1 < NCH:
            bcn = emit_stats_post_b(stn_pt)
        if not (SUB & 64) and i + 1 < NCH:
            stn = emit_stats_pre(i + 1)
            bcn = emit_stats_post(i + 1, stn)

        if not (SUB & 128):
            prev["ffn1_last_pe"] = None
            tok_act = [None] * FT
            sin_toks = [None] * FT
            q_toks = [None] * FT
        sin_toks = [None] * FT
        q_toks = [None] * FT
        tok_act = [None] * FT
        for ft in (range(FT) if SUB & 128 else []):
            g = i * FT + ft
            sl = g % 4
            psh = PS_H0 if ft % 2 == 0 else PS_H1
            w1_ready = (f"w1_{sl}", (g // 4 + 1) * 16)
            if ft >= 2:
                bankfree = [sin_toks[ft - 2], q_toks[ft - 2]]
            else:
                bankfree = prev["pshfree"][ft % 2] or []
            tokh = None
            for dc in range(DC):
                last = dc == DC - 1
                wdc = [prev["xn"][dc][1]]
                if dc == 0:
                    wdc = [w1_ready] + list(bankfree) + wdc
                tokh = OP("tensor", lambda e, dc=dc, sl=sl, psh=psh, last=last:
                          e.matmul(out=psh[:, :],
                                   lhsT=sb["w1b"][:, sl * 1024 + dc * 128: sl * 1024 + (dc + 1) * 128],
                                   rhs=sb["xnbf"][:, dc * T:(dc + 1) * T],
                                   start=(dc == 0), stop=last),
                          waits=wdc, inc=last)
            w1_cons[g] = tokh
            emit_w1_load(g + 4)
            s1sl = ft % 2
            a_sin = OP("scalar", lambda e, ft=ft, psh=psh, s1sl=s1sl:
                       e.activation(out=sb["s1"][:, s1sl * T:(s1sl + 1) * T], in_=psh[:, :],
                                    func=AF.Sin, scale=0.1, bias=sb["sinb"][:, ft:ft + 1]),
                       waits=[tokh, tok_sinb, ring["s1"][s1sl]])
            sin_toks[ft] = a_sin
            d_q = OP("vector", lambda e, psh=psh, ft=ft, s1sl=s1sl:
                     e.scalar_tensor_tensor(out=sb["q"][:, s1sl * T:(s1sl + 1) * T],
                                            in0=sb["s1"][:, s1sl * T:(s1sl + 1) * T],
                                            scalar=sb["V"][:, ft:ft + 1], in1=psh[:, :],
                                            op0=ALU.mult, op1=ALU.add),
                     waits=[a_sin, tok_V, ring["q"][s1sl]])
            q_toks[ft] = d_q
            ring["s1"][s1sl] = d_q
            d_act = OP("vector", lambda e, ft=ft, s1sl=s1sl:
                       e.tensor_scalar(out=sb["act"][:, ft * T:(ft + 1) * T],
                                       in0=sb["q"][:, s1sl * T:(s1sl + 1) * T],
                                       scalar1=sb["vb1"][:, ft:ft + 1], scalar2=0.0,
                                       op0=ALU.add, op1=ALU.max),
                       waits=[d_q, prev["ffn2B"].get(ft)])
            ring["q"][s1sl] = d_act
            tok_act[ft] = d_act
        prev["pshfree"] = [[sin_toks[FT - 2], q_toks[FT - 2]],
                           [sin_toks[FT - 1], q_toks[FT - 1]]]
        prev["ffn1_last_pe"] = w1_cons[i * FT + FT - 1]

        for dc in range(DC):
            ring["xn"][dc] = prev["ffn1_last_pe"]

        new_ffn2B = {}
        nxt_xn = [None] * DC
        for pas in (range(2) if SUB & 256 else []):
            pef = None
            for fc in range(FT):
                g2 = i * 2 * FT + pas * FT + fc
                sl = g2 % 4
                w2_ready = (f"w2_{sl}", (g2 // 4 + 1) * 16)
                w0 = [w2_ready, tok_act[fc]]
                if fc == 0:
                    w0 = w0 + [prev["silu_bank"][j] for j in range(4)]
                lastmm = None
                for j in range(4):
                    dt = pas * 4 + j
                    inc = (j == 3)
                    lastmm = OP("tensor", lambda e, j=j, sl=sl, dt=dt, fc=fc:
                                e.matmul(out=PS_F[j][:, :],
                                         lhsT=sb["w2b"][:, sl * 1024 + dt * 128: sl * 1024 + (dt + 1) * 128],
                                         rhs=sb["act"][:, fc * T:(fc + 1) * T],
                                         start=(fc == 0), stop=(fc == FT - 1)),
                                waits=(w0 if j == 0 else ()), inc=inc)
                w2_cons[g2] = lastmm
                emit_w2_load(g2 + 4)
                if pas == 1:
                    new_ffn2B[fc] = lastmm
                if fc == FT - 1:
                    pef = lastmm
            for j in range(4):
                dt = pas * 4 + j
                gsl = 0
                a_g = OP("scalar", lambda e, dt=dt, gsl=gsl, j=j:
                         e.activation(out=sb["g"][:, gsl * T:(gsl + 1) * T], in_=PS_F[j][:, :],
                                      func=AF.Silu, bias=sb["vb2"][:, dt:dt + 1]),
                         waits=[pef, ring["g"][gsl]])
                prev["silu_bank"][j] = a_g
                qsl = j % 2
                d_p = OP("vector", lambda e, dt=dt, gsl=gsl, qsl=qsl:
                         e.tensor_mul(out=sb["q"][:, qsl * T:(qsl + 1) * T],
                                      in0=sb["g"][:, gsl * T:(gsl + 1) * T],
                                      in1=sb["s"][:, dt * T:(dt + 1) * T]),
                         waits=[a_g, prev["scan"][dt], ring["q"][qsl]])
                ring["g"][gsl] = d_p
                osl = dt % 2
                ou_occ[osl] += 1
                xt_ap = sb["xt"][:, dt * T:(dt + 1) * T]
                d_o = OP("vector", lambda e, qsl=qsl, osl=osl, x2=xt_ap:
                         e.tensor_add(out=sb["outb"][:, osl * T:(osl + 1) * T],
                                      in0=sb["q"][:, qsl * T:(qsl + 1) * T], in1=x2),
                         waits=[d_p, (f"ou{osl}", (ou_occ[osl] - 1) * 16),
                                ring["xt_ready"][dt]])
                ring["q"][qsl] = d_o
                prev["combine"][dt] = d_o
                dst = d_out[b, dt * 128:(dt + 1) * 128, t0:t0 + T]
                DMA("gpsimd", lambda e, d2=dst, osl=osl:
                    e.dma_start(out=d2, in_=sb["outb"][:, osl * T:(osl + 1) * T]),
                    f"ou{osl}", waits=[d_o])
                if i + 1 < NCH:
                    emit_xt_slice(i + 1, dt, d_o)
                    nxt_xn[dt] = emit_xn_dc(i + 1, dt, bcn)
        prev["ffn2B"] = new_ffn2B
        if (SUB & 256) and i + 1 < NCH:
            prev["xn"] = [(o2, o3) for (o1, o2, o3) in nxt_xn]
            ring["psrt"][0] = nxt_xn[DC - 1][0]
            ring["psit"][0] = nxt_xn[DC - 1][1]

    for slx in range(2):
        W("gpsimd", (f"ou{slx}", dcnt[f"ou{slx}"]))

    with stack:
        with nc.Block() as block:
            @block.sync
            def _(e):
                for f in prog["sync"]:
                    f(e)

            @block.tensor
            def _(e):
                for f in prog["tensor"]:
                    f(e)

            @block.vector
            def _(e):
                for f in prog["vector"]:
                    f(e)

            @block.scalar
            def _(e):
                for f in prog["scalar"]:
                    f(e)

            @block.gpsimd
            def _(e):
                for f in prog["gpsimd"]:
                    f(e)

    return nc


_NC_CACHE = None


def kernel(**inputs):
    global _NC_CACHE
    x = np.asarray(inputs["L0_attn_x"], dtype=np.float32)
    wa = np.asarray(inputs["L0_attn_Wa"], dtype=np.float32)
    wx = np.asarray(inputs["L0_attn_Wx"], dtype=np.float32)
    ba = np.asarray(inputs["L0_attn_ba"], dtype=np.float32)
    bx = np.asarray(inputs["L0_attn_bx"], dtype=np.float32)
    lam = np.asarray(inputs["L0_attn_Lam"], dtype=np.float32)
    w1 = np.asarray(inputs["L0_ffn_W1"], dtype=np.float32)
    b1 = np.asarray(inputs["L0_ffn_b1"], dtype=np.float32)
    th = np.asarray(inputs["L0_ffn_quantum_phases"], dtype=np.float32)
    wr = np.asarray(inputs["L0_ffn_quantum_weights_real"], dtype=np.float32)
    wi = np.asarray(inputs["L0_ffn_quantum_weights_imag"], dtype=np.float32)
    w2 = np.asarray(inputs["L0_ffn_W2"], dtype=np.float32)
    b2 = np.asarray(inputs["L0_ffn_b2"], dtype=np.float32)

    def vpack(v, k):
        return np.ascontiguousarray(v.reshape(k, 128).T)

    common = {
        "waT": np.ascontiguousarray(wa.T), "wxT": np.ascontiguousarray(wx.T),
        "w1": np.ascontiguousarray(w1), "w2": np.ascontiguousarray(w2),
        "vba": vpack(ba, DC), "vbx": vpack(bx, DC), "vb2": vpack(b2, DC),
        "vlam": vpack(lam, DC), "vb1": vpack(b1, FT), "vth": vpack(th, FT),
        "vwr": vpack(wr, FT), "vwi": vpack(wi, FT),
        "iden": np.eye(128, dtype=np.float32),
    }
    in_maps = []
    for c in range(NCORE):
        xc = x[c * BC:(c + 1) * BC]
        m = dict(common)
        m["x"] = np.ascontiguousarray(xc)
        m["xT"] = np.ascontiguousarray(xc.transpose(0, 2, 1))
        in_maps.append(m)

    if _NC_CACHE is None:
        _NC_CACHE = build_program()
    res = run_bass_kernel_spmd(_NC_CACHE, in_maps, core_ids=list(range(NCORE)))
    out = np.empty((B, S, D), dtype=np.float32)
    for c in range(NCORE):
        out[c * BC:(c + 1) * BC] = np.asarray(
            res.results[c]["outT"], dtype=np.float32).transpose(0, 2, 1)
    return out


if __name__ == "__main__":
    rng = np.random.default_rng(0)
    ins = {
        "L0_attn_x": rng.standard_normal((B, S, D)).astype(np.float32),
        "L0_attn_Wa": (rng.standard_normal((D, D)) / 32).astype(np.float32),
        "L0_attn_Wx": (rng.standard_normal((D, D)) / 32).astype(np.float32),
        "L0_attn_ba": (rng.standard_normal(D) * 0.02).astype(np.float32),
        "L0_attn_bx": (rng.standard_normal(D) * 0.02).astype(np.float32),
        "L0_attn_Lam": (4.0 + 5 * rng.random(D)).astype(np.float32),
        "L0_ffn_W1": (rng.standard_normal((D, FF)) / 32).astype(np.float32),
        "L0_ffn_b1": (rng.standard_normal(FF) * 0.2).astype(np.float32),
        "L0_ffn_quantum_phases": (rng.standard_normal(FF) * 6.28).astype(np.float32),
        "L0_ffn_quantum_weights_real": (rng.standard_normal(FF) * 0.1).astype(np.float32),
        "L0_ffn_quantum_weights_imag": (rng.standard_normal(FF) * 0.1).astype(np.float32),
        "L0_ffn_W2": (rng.standard_normal((FF, D)) / 64).astype(np.float32),
        "L0_ffn_b2": (rng.standard_normal(D) * 0.2).astype(np.float32),
    }
    out = kernel(**ins)
    print("ran", out.shape, out.dtype, float(np.abs(out).mean()))


# version 04
# speedup vs baseline: 1.0901x; 1.0778x over previous
import numpy as np
import concourse.bass as bass
import concourse.mybir as mybir
from concourse.bass_utils import run_bass_kernel_spmd

F32 = mybir.dt.float32
BF16 = mybir.dt.bfloat16
AF = mybir.ActivationFunctionType
ALU = mybir.AluOpType
AX = mybir.AxisListType

import os
B, S, D, FF = 16, 4096, 1024, 4096
NCORE = 8
BC = B // NCORE
T = 512
NCH = BC * S // T
NCH = int(os.environ.get("KERNEL_NCH", NCH))
PHASE = int(os.environ.get("KERNEL_PHASE", 99))
SUB = int(os.environ.get("KERNEL_SUB", 1023))
DC = D // 128
FT = FF // 128
TT = T // 128
EPS = 1e-5

PI = float(np.float32(np.pi))
TWO_PI_HI = float(np.float32(2 * np.pi))
TWO_PI_LO = float(np.float32(2 * np.pi - np.float64(np.float32(2 * np.pi))))
PI_HI = float(np.float32(np.pi))
PI_LO = float(np.float32(np.pi - np.float64(np.float32(np.pi))))
HPI = float(np.float32(np.pi / 2))
HPI_HI = HPI
HPI_LO = float(np.float32(np.pi / 2 - np.float64(np.float32(np.pi / 2))))

ENGSEM = {"tensor": "pe", "vector": "dv", "scalar": "ac"}


def build_program():
    nc = bass.Bass("TRN2", target_bir_lowering=False, detect_race_conditions=False)

    d_x = nc.dram_tensor("x", [BC, S, D], F32, kind="ExternalInput").ap()
    d_xT = nc.dram_tensor("xT", [BC, D, S], F32, kind="ExternalInput").ap()
    d_waT = nc.dram_tensor("waT", [D, D], F32, kind="ExternalInput").ap()
    d_wxT = nc.dram_tensor("wxT", [D, D], F32, kind="ExternalInput").ap()
    d_w1 = nc.dram_tensor("w1", [D, FF], F32, kind="ExternalInput").ap()
    d_w2 = nc.dram_tensor("w2", [FF, D], F32, kind="ExternalInput").ap()
    d_vecs = {}
    for nme in ["vba", "vbx", "vb2", "vlam"]:
        d_vecs[nme] = nc.dram_tensor(nme, [128, DC], F32, kind="ExternalInput").ap()
    for nme in ["vb1", "vth", "vwr", "vwi"]:
        d_vecs[nme] = nc.dram_tensor(nme, [128, FT], F32, kind="ExternalInput").ap()
    d_vecs["iden"] = nc.dram_tensor("iden", [128, 128], F32, kind="ExternalInput").ap()
    d_out = nc.dram_tensor("outT", [BC, D, S], F32, kind="ExternalOutput").ap()
    d_w1s = nc.dram_tensor("w1s", [FT, D, 128], BF16, kind="Internal").ap()
    d_w2s = nc.dram_tensor("w2s", [FF, D], BF16, kind="Internal").ap()

    import contextlib
    stack = contextlib.ExitStack()
    sb = {}

    def S_(name, shape, dtype=F32):
        sb[name] = stack.enter_context(nc.sbuf_tensor("sb_" + name, list(shape), dtype))

    S_("wa", (128, DC * 1024), BF16)
    S_("wx", (128, DC * 1024), BF16)
    S_("w1b", (128, 4 * 1024), BF16)
    S_("w2b", (128, 4 * 1024), BF16)
    S_("xt", (128, DC * T))
    S_("xk", (128, 2 * 2048))
    S_("xn32", (128, DC * T))
    S_("xnbf", (128, DC * T), BF16)
    S_("s", (128, DC * T))
    S_("act", (128, FT * T), BF16)
    S_("it", (128, 4 * T))
    S_("a", (128, 4 * T))
    S_("asq", (128, 4 * T))
    S_("sq", (128, 4 * T))
    S_("s1", (128, 2 * T))
    S_("q", (128, 2 * T))
    S_("outb", (128, 2 * T))
    S_("sqs", (128, 512))
    S_("strow", (1, 1024))
    S_("ssq8", (128, 2 * TT))
    S_("stats8", (128, 8 * TT))
    for k, nme in enumerate(["ssum", "ssq", "mean", "var", "var2", "std",
                             "rstd", "mr"]):
        sb[nme] = sb["stats8"].ap()[:, k * TT:(k + 1) * TT]
    for nme in ["vba", "vbx", "vb2", "vlam", "coefh", "ba2", "bx2"]:
        S_(nme, (128, DC))
    for nme in ["vb1", "vth", "vwr", "vwi", "phi", "V", "sinb"]:
        S_(nme, (128, FT))
    S_("iden", (128, 128))
    S_("ones", (1, 128))
    S_("carry", (128, DC))

    act_f32 = sb["act"].ap().bitcast(F32)

    def stage_ap(sl):
        return act_f32[:, 4096 + sl * 1024: 4096 + (sl + 1) * 1024]

    ps = [stack.enter_context(nc.psum_tensor(f"ps{i}", [128, T], F32))
          for i in range(8)]
    PS_RT, PS_IT, PS_H0, PS_H1 = ps[0], ps[1], ps[2], ps[3]
    PS_F = ps[4:8]

    semnames = (["pe", "dv", "ac", "mis"] + [f"scr{k}" for k in range(4)]
                + ["xt"] + [f"xk{k}" for k in range(2)]
                + [f"w1_{k}" for k in range(4)] + [f"w2_{k}" for k in range(4)]
                + [f"st{k}" for k in range(4)] + [f"ou{k}" for k in range(2)])
    sems = {n: stack.enter_context(nc.semaphore(f"s_{n}")) for n in semnames}

    prog = {e: [] for e in ["sync", "tensor", "vector", "scalar", "gpsimd"]}
    cnt = {"pe": 0, "dv": 0, "ac": 0}
    dcnt = {n: 0 for n in semnames if n not in cnt}
    last_wait = {e: {} for e in prog}

    def W(eng, tok):
        if tok is None:
            return
        if isinstance(tok, (list, tuple)) and tok and isinstance(tok[0], (list, tuple, type(None))):
            for t in tok:
                W(eng, t)
            return
        key, val = tok
        if val <= 0:
            return
        lw = last_wait[eng]
        if lw.get(key, 0) >= val:
            return
        lw[key] = val
        sem = sems[key]
        prog[eng].append(lambda e, sem=sem, val=val: e.wait_ge(sem, val))

    def OP(eng, fn, waits=(), inc=True):
        for t in waits:
            W(eng, t)
        if inc:
            key = ENGSEM[eng]
            cnt[key] += 1
            tok = (key, cnt[key])
            sem = sems[key]
            prog[eng].append(lambda e, fn=fn, sem=sem: fn(e).then_inc(sem, 1))
            return tok
        prog[eng].append(lambda e, fn=fn: fn(e))
        return None

    def DMA(eng, fn, semkey, waits=()):
        for t in waits:
            W(eng, t)
        dcnt[semkey] += 16
        tok = (semkey, dcnt[semkey])
        sem = sems[semkey]
        prog[eng].append(lambda e, fn=fn, sem=sem: fn(e).then_inc(sem, 16))
        return tok

    ring = {
        "xkslot": [None, None],
        "it": [None] * 4, "a": [None] * 4, "asq": [None] * 4,
        "sq": [None] * 4, "s1": [None, None],
        "q": [None, None],
        "xt_ready": [None] * DC, "xk_ready": [None, None],
        "psrt": [None], "psit": [None], "bc": [None],
        "xn": [None] * DC,
    }
    w1_cons = {}
    w2_cons = {}
    ou_occ = [0, 0]
    state = {"w1_g": 0, "w2_g": 0}

    PRELOAD_FIRST = True
    mis_toks = []
    for nme in ["iden", "vba", "vbx", "vb2", "vlam", "vb1", "vth", "vwr", "vwi"]:
        src = d_vecs[nme]
        mis_toks.append(DMA("sync", lambda e, s2=src, t2=sb[nme]:
                            e.dma_start(out=t2[:, :], in_=s2), "mis"))
    tok_mis = ("mis", dcnt["mis"])

    if SUB & 8:
        for dc in range(DC):
            src0 = d_xT[0, dc * 128:(dc + 1) * 128, 0:T]
            dst0 = sb["xt"][:, dc * T:(dc + 1) * T]
            ring["xt_ready"][dc] = DMA(
                "gpsimd", lambda e, s2=src0, d2=dst0: e.dma_start(out=d2, in_=s2),
                "xt")
        for h in range(2):
            src0 = d_x[0, h * 256:(h + 1) * 256, :]
            src0 = src0.rearrange("(tt p) d2 -> p tt d2", p=128)
            dst0 = sb["xk"][:, h * 2048:(h + 1) * 2048]
            dst0 = dst0.rearrange("p (tt d2) -> p tt d2", d2=D)
            ring["xk_ready"][h] = DMA(
                "gpsimd", lambda e, s2=src0, d2=dst0: e.dma_start(out=d2, in_=s2),
                f"xk{h}")

    stage_jobs = []
    if PHASE < 2:
        global_stage_skip = True
    for dc in range(DC):
        stage_jobs.append(("wa", d_waT[dc * 128:(dc + 1) * 128, :], dc))
    for dc in range(DC):
        stage_jobs.append(("wx", d_wxT[dc * 128:(dc + 1) * 128, :], dc))
    for dc in range(DC):
        for qq in range(4):
            stage_jobs.append(("w1", d_w1[dc * 128:(dc + 1) * 128,
                                          qq * 1024:(qq + 1) * 1024], (dc, qq)))
    for fc in range(FT):
        stage_jobs.append(("w2", d_w2[fc * 128:(fc + 1) * 128, :], fc))

    stg_cons = [None] * 4
    bfw1_cons = [None] * 4
    bfw2_cons = [None] * 4
    w1_wr_last = [None] * 4
    w2_wr_last = [None] * 4
    tok_wa_done = None
    if PHASE < 2:
        stage_jobs = []
    for idx, (kind, src, meta) in enumerate(stage_jobs):
        sl = idx % 4
        ld = DMA("sync", lambda e, s2=src, sl=sl:
                 e.dma_start(out=stage_ap(sl), in_=s2),
                 f"st{sl}", waits=[stg_cons[sl]])
        if kind in ("wa", "wx"):
            dst, dc = sb[kind], meta
            cv = OP("vector", lambda e, dst=dst, dc=dc, sl=sl:
                    e.tensor_copy(out=dst[:, dc * 1024:(dc + 1) * 1024],
                                  in_=stage_ap(sl)),
                    waits=[ld])
            stg_cons[sl] = cv
            if kind == "wx" and meta == DC - 1:
                tok_wa_done = cv
        else:
            bfbuf = sb["w1b"] if kind == "w1" else sb["w2b"]
            bcons = bfw1_cons if kind == "w1" else bfw2_cons
            cv = OP("vector", lambda e, sl=sl, bfbuf=bfbuf:
                    e.tensor_copy(out=bfbuf[:, sl * 1024:(sl + 1) * 1024],
                                  in_=stage_ap(sl)),
                    waits=[ld, bcons[sl]])
            stg_cons[sl] = cv
            if PHASE < 3:
                bcons[sl] = None
                continue
            sem = f"scr{sl}"
            if kind == "w1":
                dc, qq = meta
                dstap = d_w1s[qq * 8:(qq + 1) * 8, dc * 128:(dc + 1) * 128, :]
                dstap = dstap.rearrange("ft p f -> p ft f")
                srcap = sb["w1b"][:, sl * 1024:(sl + 1) * 1024]
                srcap = srcap.rearrange("p (ft f) -> p ft f", f=128)
                wr = DMA("sync", lambda e, d2=dstap, s2=srcap:
                         e.dma_start(out=d2, in_=s2), sem, waits=[cv])
                w1_wr_last[sl] = wr
            else:
                fc = meta
                dstap = d_w2s[fc * 128:(fc + 1) * 128, :]
                srcap = sb["w2b"][:, sl * 1024:(sl + 1) * 1024]
                wr = DMA("sync", lambda e, d2=dstap, s2=srcap:
                         e.dma_start(out=d2, in_=s2), sem, waits=[cv])
                w2_wr_last[sl] = wr
            bcons[sl] = wr
    tok_w1s = [t for t in w1_wr_last if t]
    tok_w2s = [t for t in w2_wr_last if t]

    u1 = act_f32[:, 3968:4000]
    u2 = act_f32[:, 4000:4032]
    u3 = act_f32[:, 4032:4064]
    u4 = sb["sinb"]
    if SUB & 1:
        t_ones = OP("vector", lambda e: e.memset(sb["ones"][:, :], 1.0))
        t1 = OP("vector", lambda e: e.tensor_scalar_mul(out=u1[:, :DC], in0=sb["vlam"][:, :], scalar1=-1.0), waits=[tok_mis])
        t2 = OP("vector", lambda e: e.tensor_scalar_max(out=u2[:, :DC], in0=u1[:, :DC], scalar1=0.0), waits=[t1])
        t3 = OP("scalar", lambda e: e.activation(out=u3[:, :DC], in_=sb["vlam"][:, :], func=AF.Abs), waits=[tok_mis])
        a1 = OP("scalar", lambda e: e.activation(out=u4[:, :DC], in_=u3[:, :DC], func=AF.Exp, scale=-1.0), waits=[t3])
        a2 = OP("scalar", lambda e: e.activation(out=u3[:, :DC], in_=u4[:, :DC], func=AF.Ln, bias=1.0), waits=[a1])
        t4 = OP("vector", lambda e: e.tensor_add(out=u1[:, :DC], in0=u2[:, :DC], in1=u3[:, :DC]), waits=[t2, a2])
        tok_coef = OP("vector", lambda e: e.tensor_scalar_mul(out=sb["coefh"][:, :], in0=u1[:, :DC], scalar1=-4.0), waits=[t4])
        tok_ba2 = OP("vector", lambda e: e.tensor_scalar_mul(out=sb["ba2"][:, :], in0=sb["vba"][:, :], scalar1=0.5), waits=[tok_mis])
        tok_bx2 = OP("vector", lambda e: e.tensor_scalar_mul(out=sb["bx2"][:, :], in0=sb["vbx"][:, :], scalar1=0.5), waits=[tok_mis])

        pt = OP("vector", lambda e: e.tensor_copy(out=sb["phi"][:, :], in_=sb["vth"][:, :]), waits=[tok_mis])
        for _ in range(6):
            m1 = OP("vector", lambda e: e.tensor_single_scalar(out=u1[:, :], in_=sb["phi"][:, :], scalar=PI, op=ALU.is_gt), waits=[pt])
            p1 = OP("vector", lambda e: e.scalar_tensor_tensor(out=sb["phi"][:, :], in0=u1[:, :], scalar=-TWO_PI_HI, in1=sb["phi"][:, :], op0=ALU.mult, op1=ALU.add), waits=[m1])
            p2 = OP("vector", lambda e: e.scalar_tensor_tensor(out=sb["phi"][:, :], in0=u1[:, :], scalar=-TWO_PI_LO, in1=sb["phi"][:, :], op0=ALU.mult, op1=ALU.add), waits=[p1])
            m2 = OP("vector", lambda e: e.tensor_single_scalar(out=u1[:, :], in_=sb["phi"][:, :], scalar=-PI, op=ALU.is_lt), waits=[p2])
            p3 = OP("vector", lambda e: e.scalar_tensor_tensor(out=sb["phi"][:, :], in0=u1[:, :], scalar=TWO_PI_HI, in1=sb["phi"][:, :], op0=ALU.mult, op1=ALU.add), waits=[m2])
            pt = OP("vector", lambda e: e.scalar_tensor_tensor(out=sb["phi"][:, :], in0=u1[:, :], scalar=TWO_PI_LO, in1=sb["phi"][:, :], op0=ALU.mult, op1=ALU.add), waits=[p3])
        mh = OP("vector", lambda e: e.tensor_single_scalar(out=u1[:, :], in_=sb["phi"][:, :], scalar=HPI, op=ALU.is_gt), waits=[pt])
        ml = OP("vector", lambda e: e.tensor_single_scalar(out=u2[:, :], in_=sb["phi"][:, :], scalar=-HPI, op=ALU.is_lt), waits=[mh])
        r1 = OP("vector", lambda e: e.scalar_tensor_tensor(out=sb["phi"][:, :], in0=u1[:, :], scalar=-PI_HI, in1=sb["phi"][:, :], op0=ALU.mult, op1=ALU.add), waits=[ml])
        r2 = OP("vector", lambda e: e.scalar_tensor_tensor(out=sb["phi"][:, :], in0=u1[:, :], scalar=-PI_LO, in1=sb["phi"][:, :], op0=ALU.mult, op1=ALU.add), waits=[r1])
        r3 = OP("vector", lambda e: e.scalar_tensor_tensor(out=sb["phi"][:, :], in0=u2[:, :], scalar=PI_HI, in1=sb["phi"][:, :], op0=ALU.mult, op1=ALU.add), waits=[r2])
        tok_phi = OP("vector", lambda e: e.scalar_tensor_tensor(out=sb["phi"][:, :], in0=u2[:, :], scalar=PI_LO, in1=sb["phi"][:, :], op0=ALU.mult, op1=ALU.add), waits=[r3])
        s_sin = OP("scalar", lambda e: e.activation(out=u1[:, :], in_=sb["phi"][:, :], func=AF.Sin), waits=[tok_phi])
        c1 = OP("vector", lambda e: e.tensor_scalar_add(out=u2[:, :], in0=sb["phi"][:, :], scalar1=HPI_HI), waits=[tok_phi])
        c2 = OP("vector", lambda e: e.tensor_scalar_add(out=u2[:, :], in0=u2[:, :], scalar1=HPI_LO), waits=[c1])
        s_cos = OP("scalar", lambda e: e.activation(out=u3[:, :], in_=u2[:, :], func=AF.Sin), waits=[c2])
        v1 = OP("vector", lambda e: e.tensor_mul(out=u2[:, :], in0=u3[:, :], in1=sb["vwr"][:, :]), waits=[s_cos])
        v2 = OP("vector", lambda e: e.tensor_mul(out=u4[:, :], in0=u1[:, :], in1=sb["vwi"][:, :]), waits=[s_sin, v1])
        v3 = OP("vector", lambda e: e.tensor_add(out=u2[:, :], in0=u2[:, :], in1=u4[:, :]), waits=[v2])
        v4 = OP("vector", lambda e: e.tensor_mul(out=u2[:, :], in0=u2[:, :], in1=sb["vwi"][:, :]), waits=[v3])
        tok_V = OP("vector", lambda e: e.tensor_scalar_mul(out=sb["V"][:, :], in0=u2[:, :], scalar1=0.1), waits=[v4])
        b11 = OP("vector", lambda e: e.tensor_scalar_mul(out=u4[:, :], in0=sb["vb1"][:, :], scalar1=0.1), waits=[v2])
        tok_sinb = OP("vector", lambda e: e.tensor_add(out=sb["sinb"][:, :], in0=sb["phi"][:, :], in1=u4[:, :]), waits=[b11])


    else:
        t_ones = tok_coef = tok_ba2 = tok_bx2 = tok_V = tok_sinb = None

    def chunk_bt(i):
        return i // (S // T), (i % (S // T)) * T

    def emit_xt_slice(i, dc, free_tok):
        b, t0 = chunk_bt(i)
        src = d_xT[b, dc * 128:(dc + 1) * 128, t0:t0 + T]
        dst = sb["xt"][:, dc * T:(dc + 1) * T]
        ring["xt_ready"][dc] = DMA(
            "gpsimd", lambda e, s2=src, d2=dst: e.dma_start(out=d2, in_=s2),
            "xt", waits=[free_tok])

    def emit_xk_loads(i):
        b, t0 = chunk_bt(i)
        for h in range(2):
            src = d_x[b, t0 + h * 256:t0 + (h + 1) * 256, :]
            src = src.rearrange("(tt p) d2 -> p tt d2", p=128)
            dst = sb["xk"][:, h * 2048:(h + 1) * 2048]
            dst = dst.rearrange("p (tt d2) -> p tt d2", d2=D)
            ring["xk_ready"][h] = DMA(
                "gpsimd", lambda e, s2=src, d2=dst: e.dma_start(out=d2, in_=s2),
                f"xk{h}", waits=[ring["xkslot"][h]])

    def emit_w1_load(g):
        if g >= NCH * FT:
            return
        ft = g % FT
        sl = g % 4
        src = d_w1s[ft].rearrange("(dc p) f -> p dc f", p=128)
        dst = sb["w1b"][:, sl * 1024:(sl + 1) * 1024]
        dst = dst.rearrange("p (dc f) -> p dc f", f=128)
        waits = [w1_cons.get(g - 4)] if g >= 4 else list(tok_w1s)
        DMA("sync", lambda e, s2=src, d2=dst: e.dma_start(out=d2, in_=s2),
            f"w1_{sl}", waits=waits)

    def emit_w2_load(g):
        if g >= NCH * 2 * FT:
            return
        fc = g % FT
        sl = g % 4
        src = d_w2s[fc * 128:(fc + 1) * 128, :]
        dst = sb["w2b"][:, sl * 1024:(sl + 1) * 1024]
        waits = [w2_cons.get(g - 4)] if g >= 4 else list(tok_w2s)
        DMA("sync", lambda e, s2=src, d2=dst: e.dma_start(out=d2, in_=s2),
            f"w2_{sl}", waits=waits)

    def emit_stats_pre(i):
        sq_toks = []
        rd_toks = []
        for tt in range(TT):
            sl = tt // 2
            col = (tt % 2) * 1024
            for hh in range(2):
                xkap = sb["xk"][:, sl * 2048 + col + hh * 512:
                               sl * 2048 + col + (hh + 1) * 512]
                sq_toks.append(OP(
                    "scalar", lambda e, x2=xkap, tt=tt, hh=hh:
                    e.activation(out=sb["sqs"][:, :], in_=x2, func=AF.Square,
                                 accum_out=sb["ssq8"][:, hh * TT + tt:hh * TT + tt + 1]),
                    waits=[ring["xk_ready"][sl]]))
            xkap = sb["xk"][:, sl * 2048 + col: sl * 2048 + col + 1024]
            rd_toks.append(OP(
                "vector", lambda e, x2=xkap, tt=tt:
                e.reduce_sum(out=sb["ssum"][:, tt:tt + 1], in_=x2, axis=AX.X),
                waits=[ring["xk_ready"][sl]]))
        ring["xkslot"][0] = [rd_toks[1], sq_toks[3]]
        ring["xkslot"][1] = [rd_toks[3], sq_toks[7]]
        mc = OP("vector", lambda e: e.tensor_add(
            out=sb["ssq"][:, :], in0=sb["ssq8"][:, 0:TT], in1=sb["ssq8"][:, TT:2 * TT]),
            waits=[sq_toks[-1]])
        m1 = OP("vector", lambda e: e.tensor_scalar_mul(out=sb["mean"][:, :], in0=sb["ssum"][:, :], scalar1=1.0 / D),
                waits=[rd_toks[-1]])
        m2 = OP("vector", lambda e: e.tensor_mul(out=sb["var"][:, :], in0=sb["mean"][:, :], in1=sb["mean"][:, :]),
                waits=[m1])
        m3 = OP("vector", lambda e: e.scalar_tensor_tensor(out=sb["var2"][:, :], in0=sb["ssq"][:, :], scalar=1.0 / D, in1=sb["var"][:, :], op0=ALU.mult, op1=ALU.subtract),
                waits=[m2, mc])
        m4 = OP("vector", lambda e: e.tensor_scalar_add(out=sb["var2"][:, :], in0=sb["var2"][:, :], scalar1=EPS),
                waits=[m3])
        return {"var": m4}

    def emit_stats_post_a(toks):
        a_std = OP("scalar", lambda e: e.activation(out=sb["std"][:, :], in_=sb["var2"][:, :], func=AF.Sqrt),
                   waits=[toks["var"]])
        d_rs = OP("vector", lambda e: e.reciprocal(out=sb["rstd"][:, :], in_=sb["std"][:, :]),
                  waits=[a_std])
        d_mr = OP("vector", lambda e: e.tensor_mul(out=sb["mr"][:, :], in0=sb["mean"][:, :], in1=sb["rstd"][:, :]),
                  waits=[d_rs])
        return {"rs": d_rs, "mr": d_mr}

    def emit_stats_post_b(pt):
        d_rs, d_mr = pt["rs"], pt["mr"]
        tr_toks = []
        for tt in range(TT):
            tr_toks.append(OP(
                "tensor", lambda e, tt=tt:
                e.transpose(out=PS_RT[0:1, tt * 128:(tt + 1) * 128],
                            in_=sb["rstd"][:, tt:tt + 1], identity=sb["iden"][:, :]),
                waits=[d_rs, ring["psrt"][0]]))
        for tt in range(TT):
            tr_toks.append(OP(
                "tensor", lambda e, tt=tt:
                e.transpose(out=PS_IT[0:1, tt * 128:(tt + 1) * 128],
                            in_=sb["mr"][:, tt:tt + 1], identity=sb["iden"][:, :]),
                waits=[d_mr, ring["psit"][0]]))
        srow_toks = []
        for tt in range(TT):
            srow_toks.append(OP(
                "vector", lambda e, tt=tt:
                e.tensor_copy(out=sb["strow"][0:1, tt * 128:(tt + 1) * 128],
                              in_=PS_RT[0:1, tt * 128:(tt + 1) * 128]),
                waits=[tr_toks[TT - 1]]))
        for tt in range(TT):
            srow_toks.append(OP(
                "vector", lambda e, tt=tt:
                e.tensor_copy(out=sb["strow"][0:1, 512 + tt * 128:512 + (tt + 1) * 128],
                              in_=PS_IT[0:1, tt * 128:(tt + 1) * 128]),
                waits=[tr_toks[2 * TT - 1]]))
        bc1 = OP("tensor", lambda e: e.matmul(out=PS_RT[:, :], lhsT=sb["ones"][:, :], rhs=sb["strow"][0:1, 0:512], start=True, stop=True),
                 waits=srow_toks[0:4] + [t_ones])
        bc2 = OP("tensor", lambda e: e.matmul(out=PS_IT[:, :], lhsT=sb["ones"][:, :], rhs=sb["strow"][0:1, 512:1024], start=True, stop=True),
                 waits=srow_toks[4:8])
        e1 = OP("vector", lambda e: e.tensor_copy(out=sb["a"][:, 0:T], in_=PS_RT[:, :]),
                waits=[bc1, ring["a"][0]])
        e2 = OP("vector", lambda e: e.tensor_copy(out=sb["a"][:, T:2 * T], in_=PS_IT[:, :]),
                waits=[bc2, ring["a"][1]])
        ring["psrt"][0] = e1
        ring["psit"][0] = e2
        return e1, e2

    def emit_stats_post(i, toks):
        return emit_stats_post_b(emit_stats_post_a(toks))

    prev = {
        "xn": None, "combine": [None] * DC, "scan": [None] * DC,
        "carry": [None] * DC, "silu_bank": [None] * 4,
        "ffn1_last_pe": None, "ffn2B": {}, "pshfree": [None, None],
    }

    def emit_xn_dc(i, dc, bc_toks):
        bc1, bc2 = bc_toks
        xt_ap = sb["xt"][:, dc * T:(dc + 1) * T]
        w0 = [ring["xt_ready"][dc], bc1, bc2, ring["xn"][dc], prev["scan"][dc]]
        o1 = OP("vector", lambda e, x2=xt_ap, dc=dc:
                e.tensor_mul(out=sb["xn32"][:, dc * T:(dc + 1) * T], in0=x2, in1=sb["a"][:, 0:T]),
                waits=w0)
        o2 = OP("vector", lambda e, dc=dc:
                e.tensor_sub(out=sb["xn32"][:, dc * T:(dc + 1) * T],
                             in0=sb["xn32"][:, dc * T:(dc + 1) * T], in1=sb["a"][:, T:2 * T]),
                waits=[o1])
        o3 = OP("vector", lambda e, dc=dc:
                e.tensor_copy(out=sb["xnbf"][:, dc * T:(dc + 1) * T],
                              in_=sb["xn32"][:, dc * T:(dc + 1) * T]),
                waits=[o2])
        return (o1, o2, o3)

    if SUB & 4:
        for g in range(4):
            emit_w1_load(g)
            emit_w2_load(g)
    if SUB & 2:
        st0 = emit_stats_pre(0)
    if SUB & 16:
        bc0 = emit_stats_post(0, st0)
    if SUB & 32:
        _xn0 = [emit_xn_dc(0, dc, bc0) for dc in range(DC)]
        prev["xn"] = [(o2, o3) for (o1, o2, o3) in _xn0]
        ring["psrt"][0] = _xn0[-1][0]
        ring["psit"][0] = _xn0[-1][1]

    for i in range(NCH):
        b, t0 = chunk_bt(i)
        batch_start = (i % (S // T)) == 0

        if i + 1 < NCH:
            emit_xk_loads(i + 1)

        tok_a = [None] * DC
        stn = None
        for half in (range(2) if SUB & 64 else []):
            dts = list(range(half * 4, half * 4 + 4))
            tok_it = {}
            tok_cl = {}
            for dt in dts:
                asl = dt % 4
                tokr = None
                for dc in range(DC):
                    last = dc == DC - 1
                    wdc = [prev["xn"][dc][1]]
                    if dc == 0:
                        wdc += [ring["psrt"][0], tok_wa_done]
                    tokr = OP("tensor", lambda e, dc=dc, dt=dt, last=last:
                              e.matmul(out=PS_RT[:, :],
                                       lhsT=sb["wa"][:, dc * 1024 + dt * 128: dc * 1024 + (dt + 1) * 128],
                                       rhs=sb["xnbf"][:, dc * T:(dc + 1) * T],
                                       start=(dc == 0), stop=last),
                              waits=wdc, inc=last)
                pe_rt = tokr
                toki = None
                for dc in range(DC):
                    last = dc == DC - 1
                    toki = OP("tensor", lambda e, dc=dc, dt=dt, last=last:
                              e.matmul(out=PS_IT[:, :],
                                       lhsT=sb["wx"][:, dc * 1024 + dt * 128: dc * 1024 + (dt + 1) * 128],
                                       rhs=sb["xnbf"][:, dc * T:(dc + 1) * T],
                                       start=(dc == 0), stop=last),
                              waits=([ring["psit"][0]] if dc == 0 else ()), inc=last)
                pe_it = toki
                a_th = OP("scalar", lambda e, dt=dt:
                          e.activation(out=sb["sqs"][:, :], in_=PS_RT[:, :], func=AF.Tanh,
                                       scale=0.5, bias=sb["ba2"][:, dt:dt + 1]),
                          waits=[pe_rt, tok_ba2, tok_coef])
                ring["psrt"][0] = a_th
                a_a = OP("scalar", lambda e, dt=dt, asl=asl:
                         e.activation(out=sb["a"][:, asl * T:(asl + 1) * T], in_=sb["sqs"][:, :],
                                      func=AF.Exp, scale=sb["coefh"][:, dt:dt + 1],
                                      bias=sb["coefh"][:, dt:dt + 1]),
                         waits=[a_th, ring["a"][asl]])
                tok_a[dt] = a_a
                a_ti = OP("scalar", lambda e, dt=dt, asl=asl:
                          e.activation(out=sb["it"][:, asl * T:(asl + 1) * T], in_=PS_IT[:, :],
                                       func=AF.Tanh, scale=0.5, bias=sb["bx2"][:, dt:dt + 1]),
                          waits=[pe_it, tok_bx2, ring["it"][asl]])
                ring["psit"][0] = a_ti
                a_sq = OP("scalar", lambda e, asl=asl:
                          e.activation(out=sb["asq"][:, asl * T:(asl + 1) * T],
                                       in_=sb["a"][:, asl * T:(asl + 1) * T], func=AF.Square),
                          waits=[a_a, ring["asq"][asl]])
                d_it = OP("vector", lambda e, asl=asl:
                          e.tensor_scalar(out=sb["it"][:, asl * T:(asl + 1) * T],
                                          in0=sb["it"][:, asl * T:(asl + 1) * T],
                                          scalar1=0.5, scalar2=0.5, op0=ALU.mult, op1=ALU.add),
                          waits=[a_ti])
                tok_it[dt] = d_it
                tok_cl[dt] = OP("vector", lambda e, asl=asl:
                                e.tensor_single_scalar(out=sb["asq"][:, asl * T:(asl + 1) * T],
                                                       in_=sb["asq"][:, asl * T:(asl + 1) * T],
                                                       scalar=1.0, op=ALU.min),
                                waits=[a_sq])

            if half == 1 and i + 1 < NCH:
                stn = emit_stats_pre(i + 1)

            tok_s = {}
            for dt in dts:
                asl = dt % 4
                tok_s[dt] = OP("scalar", lambda e, asl=asl:
                               e.activation(out=sb["sq"][:, asl * T:(asl + 1) * T],
                                            in_=sb["asq"][:, asl * T:(asl + 1) * T],
                                            func=AF.Sqrt, scale=-1.0, bias=1.0),
                               waits=[tok_cl[dt], ring["sq"][asl]])
            if half == 1 and i + 1 < NCH:
                stn_pt = emit_stats_post_a(stn)

            for dt in dts:
                asl = dt % 4
                d_t2 = OP("vector", lambda e, asl=asl, dt=dt:
                          e.tensor_mul(out=sb["asq"][:, asl * T:(asl + 1) * T],
                                       in0=sb["sq"][:, asl * T:(asl + 1) * T],
                                       in1=sb["xn32"][:, dt * T:(dt + 1) * T]),
                          waits=[tok_s[dt], prev["xn"][dt][0]])
                ring["sq"][asl] = d_t2
                d_b = OP("vector", lambda e, asl=asl:
                         e.tensor_mul(out=sb["asq"][:, asl * T:(asl + 1) * T],
                                      in0=sb["it"][:, asl * T:(asl + 1) * T],
                                      in1=sb["asq"][:, asl * T:(asl + 1) * T]),
                         waits=[d_t2, tok_it[dt]])
                ring["it"][asl] = d_b
                init = 0.0 if batch_start else sb["carry"][:, dt:dt + 1]
                d_sc = OP("vector", lambda e, asl=asl, dt=dt, init=init:
                          e.tensor_tensor_scan(out=sb["s"][:, dt * T:(dt + 1) * T],
                                               data0=sb["a"][:, asl * T:(asl + 1) * T],
                                               data1=sb["asq"][:, asl * T:(asl + 1) * T],
                                               initial=init, op0=ALU.mult, op1=ALU.add),
                          waits=[d_b, tok_a[dt], prev["combine"][dt], prev["carry"][dt]])
                ring["a"][asl] = d_sc
                ring["asq"][asl] = d_sc
                prev["scan"][dt] = d_sc
                prev["carry"][dt] = OP(
                    "vector", lambda e, dt=dt:
                    e.tensor_copy(out=sb["carry"][:, dt:dt + 1],
                                  in_=sb["s"][:, dt * T + T - 1: dt * T + T]),
                    waits=[d_sc])

        if (SUB & 64) and i + 1 < NCH:
            bcn = emit_stats_post_b(stn_pt)
        if not (SUB & 64) and i + 1 < NCH:
            stn = emit_stats_pre(i + 1)
            bcn = emit_stats_post(i + 1, stn)

        if not (SUB & 128):
            prev["ffn1_last_pe"] = None
            tok_act = [None] * FT
            sin_toks = [None] * FT
            q_toks = [None] * FT
        sin_toks = [None] * FT
        q_toks = [None] * FT
        tok_act = [None] * FT
        for ft in (range(FT) if SUB & 128 else []):
            g = i * FT + ft
            sl = g % 4
            psh = PS_H0 if ft % 2 == 0 else PS_H1
            w1_ready = (f"w1_{sl}", (g // 4 + 1) * 16)
            if ft >= 2:
                bankfree = [sin_toks[ft - 2], q_toks[ft - 2]]
            else:
                bankfree = prev["pshfree"][ft % 2] or []
            tokh = None
            for dc in range(DC):
                last = dc == DC - 1
                wdc = [prev["xn"][dc][1]]
                if dc == 0:
                    wdc = [w1_ready] + list(bankfree) + wdc
                tokh = OP("tensor", lambda e, dc=dc, sl=sl, psh=psh, last=last:
                          e.matmul(out=psh[:, :],
                                   lhsT=sb["w1b"][:, sl * 1024 + dc * 128: sl * 1024 + (dc + 1) * 128],
                                   rhs=sb["xnbf"][:, dc * T:(dc + 1) * T],
                                   start=(dc == 0), stop=last),
                          waits=wdc, inc=last)
            w1_cons[g] = tokh
            emit_w1_load(g + 4)
            s1sl = ft % 2
            a_sin = OP("scalar", lambda e, ft=ft, psh=psh, s1sl=s1sl:
                       e.activation(out=sb["s1"][:, s1sl * T:(s1sl + 1) * T], in_=psh[:, :],
                                    func=AF.Sin, scale=0.1, bias=sb["sinb"][:, ft:ft + 1]),
                       waits=[tokh, tok_sinb, ring["s1"][s1sl]])
            sin_toks[ft] = a_sin
            d_q = OP("vector", lambda e, psh=psh, ft=ft, s1sl=s1sl:
                     e.scalar_tensor_tensor(out=sb["q"][:, s1sl * T:(s1sl + 1) * T],
                                            in0=sb["s1"][:, s1sl * T:(s1sl + 1) * T],
                                            scalar=sb["V"][:, ft:ft + 1], in1=psh[:, :],
                                            op0=ALU.mult, op1=ALU.add),
                     waits=[a_sin, tok_V, ring["q"][s1sl]])
            q_toks[ft] = d_q
            ring["s1"][s1sl] = d_q
            d_act = OP("vector", lambda e, ft=ft, s1sl=s1sl:
                       e.tensor_scalar(out=sb["act"][:, ft * T:(ft + 1) * T],
                                       in0=sb["q"][:, s1sl * T:(s1sl + 1) * T],
                                       scalar1=sb["vb1"][:, ft:ft + 1], scalar2=0.0,
                                       op0=ALU.add, op1=ALU.max),
                       waits=[d_q, prev["ffn2B"].get(ft)])
            ring["q"][s1sl] = d_act
            tok_act[ft] = d_act
        prev["pshfree"] = [[sin_toks[FT - 2], q_toks[FT - 2]],
                           [sin_toks[FT - 1], q_toks[FT - 1]]]
        prev["ffn1_last_pe"] = w1_cons[i * FT + FT - 1]

        for dc in range(DC):
            ring["xn"][dc] = prev["ffn1_last_pe"]

        new_ffn2B = {}
        nxt_xn = [None] * DC
        for pas in (range(2) if SUB & 256 else []):
            pef = None
            for fc in range(FT):
                g2 = i * 2 * FT + pas * FT + fc
                sl = g2 % 4
                w2_ready = (f"w2_{sl}", (g2 // 4 + 1) * 16)
                w0 = [w2_ready, tok_act[fc]]
                if fc == 0:
                    w0 = w0 + [prev["silu_bank"][j] for j in range(4)]
                lastmm = None
                for j in range(4):
                    dt = pas * 4 + j
                    inc = (j == 3)
                    lastmm = OP("tensor", lambda e, j=j, sl=sl, dt=dt, fc=fc:
                                e.matmul(out=PS_F[j][:, :],
                                         lhsT=sb["w2b"][:, sl * 1024 + dt * 128: sl * 1024 + (dt + 1) * 128],
                                         rhs=sb["act"][:, fc * T:(fc + 1) * T],
                                         start=(fc == 0), stop=(fc == FT - 1)),
                                waits=(w0 if j == 0 else ()), inc=inc)
                w2_cons[g2] = lastmm
                emit_w2_load(g2 + 4)
                if pas == 1:
                    new_ffn2B[fc] = lastmm
                if fc == FT - 1:
                    pef = lastmm
            for j in range(4):
                dt = pas * 4 + j
                qsl = j % 2
                a_g = OP("scalar", lambda e, dt=dt, qsl=qsl, j=j:
                         e.activation(out=sb["q"][:, qsl * T:(qsl + 1) * T], in_=PS_F[j][:, :],
                                      func=AF.Silu, bias=sb["vb2"][:, dt:dt + 1]),
                         waits=[pef, ring["q"][qsl]])
                prev["silu_bank"][j] = a_g
                d_p = OP("vector", lambda e, dt=dt, qsl=qsl:
                         e.tensor_mul(out=sb["q"][:, qsl * T:(qsl + 1) * T],
                                      in0=sb["q"][:, qsl * T:(qsl + 1) * T],
                                      in1=sb["s"][:, dt * T:(dt + 1) * T]),
                         waits=[a_g, prev["scan"][dt]])
                osl = dt % 2
                ou_occ[osl] += 1
                xt_ap = sb["xt"][:, dt * T:(dt + 1) * T]
                d_o = OP("vector", lambda e, qsl=qsl, osl=osl, x2=xt_ap:
                         e.tensor_add(out=sb["outb"][:, osl * T:(osl + 1) * T],
                                      in0=sb["q"][:, qsl * T:(qsl + 1) * T], in1=x2),
                         waits=[d_p, (f"ou{osl}", (ou_occ[osl] - 1) * 16),
                                ring["xt_ready"][dt]])
                ring["q"][qsl] = d_o
                prev["combine"][dt] = d_o
                dst = d_out[b, dt * 128:(dt + 1) * 128, t0:t0 + T]
                DMA("gpsimd", lambda e, d2=dst, osl=osl:
                    e.dma_start(out=d2, in_=sb["outb"][:, osl * T:(osl + 1) * T]),
                    f"ou{osl}", waits=[d_o])
                if i + 1 < NCH:
                    emit_xt_slice(i + 1, dt, d_o)
                    nxt_xn[dt] = emit_xn_dc(i + 1, dt, bcn)
        prev["ffn2B"] = new_ffn2B
        if (SUB & 256) and i + 1 < NCH:
            prev["xn"] = [(o2, o3) for (o1, o2, o3) in nxt_xn]
            ring["a"][0] = nxt_xn[DC - 1][0]
            ring["a"][1] = nxt_xn[DC - 1][1]

    for slx in range(2):
        W("gpsimd", (f"ou{slx}", dcnt[f"ou{slx}"]))

    with stack:
        with nc.Block() as block:
            @block.sync
            def _(e):
                for f in prog["sync"]:
                    f(e)

            @block.tensor
            def _(e):
                for f in prog["tensor"]:
                    f(e)

            @block.vector
            def _(e):
                for f in prog["vector"]:
                    f(e)

            @block.scalar
            def _(e):
                for f in prog["scalar"]:
                    f(e)

            @block.gpsimd
            def _(e):
                for f in prog["gpsimd"]:
                    f(e)

    return nc


_NC_CACHE = None


def kernel(**inputs):
    global _NC_CACHE
    x = np.asarray(inputs["L0_attn_x"], dtype=np.float32)
    wa = np.asarray(inputs["L0_attn_Wa"], dtype=np.float32)
    wx = np.asarray(inputs["L0_attn_Wx"], dtype=np.float32)
    ba = np.asarray(inputs["L0_attn_ba"], dtype=np.float32)
    bx = np.asarray(inputs["L0_attn_bx"], dtype=np.float32)
    lam = np.asarray(inputs["L0_attn_Lam"], dtype=np.float32)
    w1 = np.asarray(inputs["L0_ffn_W1"], dtype=np.float32)
    b1 = np.asarray(inputs["L0_ffn_b1"], dtype=np.float32)
    th = np.asarray(inputs["L0_ffn_quantum_phases"], dtype=np.float32)
    wr = np.asarray(inputs["L0_ffn_quantum_weights_real"], dtype=np.float32)
    wi = np.asarray(inputs["L0_ffn_quantum_weights_imag"], dtype=np.float32)
    w2 = np.asarray(inputs["L0_ffn_W2"], dtype=np.float32)
    b2 = np.asarray(inputs["L0_ffn_b2"], dtype=np.float32)

    def vpack(v, k):
        return np.ascontiguousarray(v.reshape(k, 128).T)

    common = {
        "waT": np.ascontiguousarray(wa.T), "wxT": np.ascontiguousarray(wx.T),
        "w1": np.ascontiguousarray(w1), "w2": np.ascontiguousarray(w2),
        "vba": vpack(ba, DC), "vbx": vpack(bx, DC), "vb2": vpack(b2, DC),
        "vlam": vpack(lam, DC), "vb1": vpack(b1, FT), "vth": vpack(th, FT),
        "vwr": vpack(wr, FT), "vwi": vpack(wi, FT),
        "iden": np.eye(128, dtype=np.float32),
    }
    in_maps = []
    for c in range(NCORE):
        xc = x[c * BC:(c + 1) * BC]
        m = dict(common)
        m["x"] = np.ascontiguousarray(xc)
        m["xT"] = np.ascontiguousarray(xc.transpose(0, 2, 1))
        in_maps.append(m)

    if _NC_CACHE is None:
        _NC_CACHE = build_program()
    res = run_bass_kernel_spmd(_NC_CACHE, in_maps, core_ids=list(range(NCORE)))
    out = np.empty((B, S, D), dtype=np.float32)
    for c in range(NCORE):
        out[c * BC:(c + 1) * BC] = np.asarray(
            res.results[c]["outT"], dtype=np.float32).transpose(0, 2, 1)
    return out


if __name__ == "__main__":
    rng = np.random.default_rng(0)
    ins = {
        "L0_attn_x": rng.standard_normal((B, S, D)).astype(np.float32),
        "L0_attn_Wa": (rng.standard_normal((D, D)) / 32).astype(np.float32),
        "L0_attn_Wx": (rng.standard_normal((D, D)) / 32).astype(np.float32),
        "L0_attn_ba": (rng.standard_normal(D) * 0.02).astype(np.float32),
        "L0_attn_bx": (rng.standard_normal(D) * 0.02).astype(np.float32),
        "L0_attn_Lam": (4.0 + 5 * rng.random(D)).astype(np.float32),
        "L0_ffn_W1": (rng.standard_normal((D, FF)) / 32).astype(np.float32),
        "L0_ffn_b1": (rng.standard_normal(FF) * 0.2).astype(np.float32),
        "L0_ffn_quantum_phases": (rng.standard_normal(FF) * 6.28).astype(np.float32),
        "L0_ffn_quantum_weights_real": (rng.standard_normal(FF) * 0.1).astype(np.float32),
        "L0_ffn_quantum_weights_imag": (rng.standard_normal(FF) * 0.1).astype(np.float32),
        "L0_ffn_W2": (rng.standard_normal((FF, D)) / 64).astype(np.float32),
        "L0_ffn_b2": (rng.standard_normal(D) * 0.2).astype(np.float32),
    }
    out = kernel(**ins)
    print("ran", out.shape, out.dtype, float(np.abs(out).mean()))


# version 05
# speedup vs baseline: 1.1164x; 1.0241x over previous
import numpy as np
import concourse.bass as bass
import concourse.mybir as mybir
from concourse.bass_utils import run_bass_kernel_spmd

F32 = mybir.dt.float32
BF16 = mybir.dt.bfloat16
AF = mybir.ActivationFunctionType
ALU = mybir.AluOpType
AX = mybir.AxisListType

import os
B, S, D, FF = 16, 4096, 1024, 4096
NCORE = 8
BC = B // NCORE
T = 512
NCH = BC * S // T
NCH = int(os.environ.get("KERNEL_NCH", NCH))
PHASE = int(os.environ.get("KERNEL_PHASE", 99))
SUB = int(os.environ.get("KERNEL_SUB", 1023))
DC = D // 128
FT = FF // 128
TT = T // 128
EPS = 1e-5

PI = float(np.float32(np.pi))
TWO_PI_HI = float(np.float32(2 * np.pi))
TWO_PI_LO = float(np.float32(2 * np.pi - np.float64(np.float32(2 * np.pi))))
PI_HI = float(np.float32(np.pi))
PI_LO = float(np.float32(np.pi - np.float64(np.float32(np.pi))))
HPI = float(np.float32(np.pi / 2))
HPI_HI = HPI
HPI_LO = float(np.float32(np.pi / 2 - np.float64(np.float32(np.pi / 2))))

ENGSEM = {"tensor": "pe", "vector": "dv", "scalar": "ac"}


def build_program():
    nc = bass.Bass("TRN2", target_bir_lowering=False, detect_race_conditions=False)

    d_x = nc.dram_tensor("x", [BC, S, D], F32, kind="ExternalInput").ap()
    d_xT = nc.dram_tensor("xT", [BC, D, S], F32, kind="ExternalInput").ap()
    d_waT = nc.dram_tensor("waT", [D, D], F32, kind="ExternalInput").ap()
    d_wxT = nc.dram_tensor("wxT", [D, D], F32, kind="ExternalInput").ap()
    d_w1 = nc.dram_tensor("w1", [D, FF], F32, kind="ExternalInput").ap()
    d_w2 = nc.dram_tensor("w2", [FF, D], F32, kind="ExternalInput").ap()
    d_vecs = {}
    for nme in ["vba", "vbx", "vb2", "vlam"]:
        d_vecs[nme] = nc.dram_tensor(nme, [128, DC], F32, kind="ExternalInput").ap()
    for nme in ["vb1", "vth", "vwr", "vwi"]:
        d_vecs[nme] = nc.dram_tensor(nme, [128, FT], F32, kind="ExternalInput").ap()
    d_vecs["iden"] = nc.dram_tensor("iden", [128, 128], F32, kind="ExternalInput").ap()
    d_out = nc.dram_tensor("outT", [BC, D, S], F32, kind="ExternalOutput").ap()
    d_w1s = nc.dram_tensor("w1s", [FT, D, 128], BF16, kind="Internal").ap()
    d_w2s = nc.dram_tensor("w2s", [FF, D], BF16, kind="Internal").ap()

    import contextlib
    stack = contextlib.ExitStack()
    sb = {}

    def S_(name, shape, dtype=F32):
        sb[name] = stack.enter_context(nc.sbuf_tensor("sb_" + name, list(shape), dtype))

    S_("wa", (128, DC * 1024), BF16)
    S_("wx", (128, DC * 1024), BF16)
    S_("w1b", (128, 5 * 1024), BF16)
    S_("w2b", (128, 5 * 1024), BF16)
    S_("xt", (128, DC * T))
    S_("xk", (128, 2 * 2048))
    S_("xn32", (128, DC * T))
    S_("xnbf", (128, DC * T), BF16)
    S_("s", (128, DC * T))
    S_("act", (128, FT * T), BF16)
    S_("it", (128, 4 * T))
    S_("a", (128, 4 * T))
    S_("asq", (128, 4 * T))
    S_("sq", (128, 4 * T))
    S_("s1", (128, 2 * T))
    S_("q", (128, 2 * T))
    S_("outb", (128, 2 * T))
    S_("sqs", (128, 512))
    S_("strow", (1, 512))
    S_("ssq8", (128, 2 * TT))
    S_("stats8", (128, 8 * TT))
    for k, nme in enumerate(["ssum", "ssq", "mean", "var", "var2", "std",
                             "rstd", "mr"]):
        sb[nme] = sb["stats8"].ap()[:, k * TT:(k + 1) * TT]
    for nme in ["vb2", "coefh", "ba2", "bx2"]:
        S_(nme, (128, DC))
    for nme in ["vb1", "V", "sinb"]:
        S_(nme, (128, FT))
    S_("iden", (128, 128))
    S_("ones", (1, 128))
    S_("carry", (128, DC))

    act_f32 = sb["act"].ap().bitcast(F32)
    sb["vth"] = act_f32[:, 3784:3816]
    sb["vwr"] = act_f32[:, 3816:3848]
    sb["vwi"] = act_f32[:, 3848:3880]
    sb["vba"] = act_f32[:, 3880:3888]
    sb["vbx"] = act_f32[:, 3888:3896]
    sb["vlam"] = act_f32[:, 3896:3904]
    sb["phi"] = act_f32[:, 3904:3936]

    def stage_ap(sl):
        return act_f32[:, 4096 + sl * 1024: 4096 + (sl + 1) * 1024]

    ps = [stack.enter_context(nc.psum_tensor(f"ps{i}", [128, T], F32))
          for i in range(8)]
    PS_RT, PS_IT, PS_H0, PS_H1 = ps[0], ps[1], ps[2], ps[3]
    PS_F = ps[4:8]

    semnames = (["pe", "dv", "ac", "mis"] + [f"scr{k}" for k in range(4)]
                + ["xt"] + [f"xk{k}" for k in range(2)]
                + [f"w1_{k}" for k in range(5)] + [f"w2_{k}" for k in range(5)]
                + [f"st{k}" for k in range(4)] + [f"ou{k}" for k in range(2)])
    sems = {n: stack.enter_context(nc.semaphore(f"s_{n}")) for n in semnames}

    prog = {e: [] for e in ["sync", "tensor", "vector", "scalar", "gpsimd"]}
    cnt = {"pe": 0, "dv": 0, "ac": 0}
    dcnt = {n: 0 for n in semnames if n not in cnt}
    last_wait = {e: {} for e in prog}

    def W(eng, tok):
        if tok is None:
            return
        if isinstance(tok, (list, tuple)) and tok and isinstance(tok[0], (list, tuple, type(None))):
            for t in tok:
                W(eng, t)
            return
        key, val = tok
        if val <= 0:
            return
        lw = last_wait[eng]
        if lw.get(key, 0) >= val:
            return
        lw[key] = val
        sem = sems[key]
        prog[eng].append(lambda e, sem=sem, val=val: e.wait_ge(sem, val))

    def OP(eng, fn, waits=(), inc=True):
        for t in waits:
            W(eng, t)
        if inc:
            key = ENGSEM[eng]
            cnt[key] += 1
            tok = (key, cnt[key])
            sem = sems[key]
            prog[eng].append(lambda e, fn=fn, sem=sem: fn(e).then_inc(sem, 1))
            return tok
        prog[eng].append(lambda e, fn=fn: fn(e))
        return None

    def DMA(eng, fn, semkey, waits=()):
        for t in waits:
            W(eng, t)
        dcnt[semkey] += 16
        tok = (semkey, dcnt[semkey])
        sem = sems[semkey]
        prog[eng].append(lambda e, fn=fn, sem=sem: fn(e).then_inc(sem, 16))
        return tok

    ring = {
        "xkslot": [None, None],
        "it": [None] * 4, "a": [None] * 4, "asq": [None] * 4,
        "sq": [None] * 4, "s1": [None, None],
        "q": [None, None],
        "xt_ready": [None] * DC, "xk_ready": [None, None],
        "psrt": [None], "psit": [None], "bc": [None],
        "xn": [None] * DC,
    }
    w1_cons = {}
    w2_cons = {}
    ou_occ = [0, 0]
    state = {"w1_g": 0, "w2_g": 0}

    PRELOAD_FIRST = True
    mis_toks = []
    for nme in ["iden", "vba", "vbx", "vb2", "vlam", "vb1", "vth", "vwr", "vwi"]:
        src = d_vecs[nme]
        mis_toks.append(DMA("sync", lambda e, s2=src, t2=sb[nme]:
                            e.dma_start(out=t2[:, :], in_=s2), "mis"))
    tok_mis = ("mis", dcnt["mis"])

    if SUB & 8:
        for dc in range(DC):
            src0 = d_xT[0, dc * 128:(dc + 1) * 128, 0:T]
            dst0 = sb["xt"][:, dc * T:(dc + 1) * T]
            ring["xt_ready"][dc] = DMA(
                "gpsimd", lambda e, s2=src0, d2=dst0: e.dma_start(out=d2, in_=s2),
                "xt")
        for h in range(2):
            src0 = d_x[0, h * 256:(h + 1) * 256, :]
            src0 = src0.rearrange("(tt p) d2 -> p tt d2", p=128)
            dst0 = sb["xk"][:, h * 2048:(h + 1) * 2048]
            dst0 = dst0.rearrange("p (tt d2) -> p tt d2", d2=D)
            ring["xk_ready"][h] = DMA(
                "gpsimd", lambda e, s2=src0, d2=dst0: e.dma_start(out=d2, in_=s2),
                f"xk{h}")

    stage_jobs = []
    if PHASE < 2:
        global_stage_skip = True
    for dc in range(DC):
        stage_jobs.append(("wa", d_waT[dc * 128:(dc + 1) * 128, :], dc))
    for dc in range(DC):
        stage_jobs.append(("wx", d_wxT[dc * 128:(dc + 1) * 128, :], dc))
    for dc in range(DC):
        for qq in range(4):
            stage_jobs.append(("w1", d_w1[dc * 128:(dc + 1) * 128,
                                          qq * 1024:(qq + 1) * 1024], (dc, qq)))
    for fc in range(FT):
        stage_jobs.append(("w2", d_w2[fc * 128:(fc + 1) * 128, :], fc))

    stg_cons = [None] * 4
    bfw1_cons = [None] * 4
    bfw2_cons = [None] * 4
    w1_wr_last = [None] * 4
    w2_wr_last = [None] * 4
    tok_wa_done = None
    if PHASE < 2:
        stage_jobs = []
    for idx, (kind, src, meta) in enumerate(stage_jobs):
        sl = idx % 4
        ld = DMA("sync", lambda e, s2=src, sl=sl:
                 e.dma_start(out=stage_ap(sl), in_=s2),
                 f"st{sl}", waits=[stg_cons[sl]])
        if kind in ("wa", "wx"):
            dst, dc = sb[kind], meta
            cv = OP("vector", lambda e, dst=dst, dc=dc, sl=sl:
                    e.tensor_copy(out=dst[:, dc * 1024:(dc + 1) * 1024],
                                  in_=stage_ap(sl)),
                    waits=[ld])
            stg_cons[sl] = cv
            if kind == "wx" and meta == DC - 1:
                tok_wa_done = cv
        else:
            bfbuf = sb["w1b"] if kind == "w1" else sb["w2b"]
            bcons = bfw1_cons if kind == "w1" else bfw2_cons
            cv = OP("vector", lambda e, sl=sl, bfbuf=bfbuf:
                    e.tensor_copy(out=bfbuf[:, sl * 1024:(sl + 1) * 1024],
                                  in_=stage_ap(sl)),
                    waits=[ld, bcons[sl]])
            stg_cons[sl] = cv
            if PHASE < 3:
                bcons[sl] = None
                continue
            sem = f"scr{sl}"
            if kind == "w1":
                dc, qq = meta
                dstap = d_w1s[qq * 8:(qq + 1) * 8, dc * 128:(dc + 1) * 128, :]
                dstap = dstap.rearrange("ft p f -> p ft f")
                srcap = sb["w1b"][:, sl * 1024:(sl + 1) * 1024]
                srcap = srcap.rearrange("p (ft f) -> p ft f", f=128)
                wr = DMA("sync", lambda e, d2=dstap, s2=srcap:
                         e.dma_start(out=d2, in_=s2), sem, waits=[cv])
                w1_wr_last[sl] = wr
            else:
                fc = meta
                dstap = d_w2s[fc * 128:(fc + 1) * 128, :]
                srcap = sb["w2b"][:, sl * 1024:(sl + 1) * 1024]
                wr = DMA("sync", lambda e, d2=dstap, s2=srcap:
                         e.dma_start(out=d2, in_=s2), sem, waits=[cv])
                w2_wr_last[sl] = wr
            bcons[sl] = wr
    tok_w1s = [t for t in w1_wr_last if t]
    tok_w2s = [t for t in w2_wr_last if t]

    u1 = act_f32[:, 3968:4000]
    u2 = act_f32[:, 4000:4032]
    u3 = act_f32[:, 4032:4064]
    u4 = sb["sinb"]
    if SUB & 1:
        t_ones = OP("vector", lambda e: e.memset(sb["ones"][:, :], 1.0))
        t1 = OP("vector", lambda e: e.tensor_scalar_mul(out=u1[:, :DC], in0=sb["vlam"][:, :], scalar1=-1.0), waits=[tok_mis])
        t2 = OP("vector", lambda e: e.tensor_scalar_max(out=u2[:, :DC], in0=u1[:, :DC], scalar1=0.0), waits=[t1])
        t3 = OP("scalar", lambda e: e.activation(out=u3[:, :DC], in_=sb["vlam"][:, :], func=AF.Abs), waits=[tok_mis])
        a1 = OP("scalar", lambda e: e.activation(out=u4[:, :DC], in_=u3[:, :DC], func=AF.Exp, scale=-1.0), waits=[t3])
        a2 = OP("scalar", lambda e: e.activation(out=u3[:, :DC], in_=u4[:, :DC], func=AF.Ln, bias=1.0), waits=[a1])
        t4 = OP("vector", lambda e: e.tensor_add(out=u1[:, :DC], in0=u2[:, :DC], in1=u3[:, :DC]), waits=[t2, a2])
        tok_coef = OP("vector", lambda e: e.tensor_scalar_mul(out=sb["coefh"][:, :], in0=u1[:, :DC], scalar1=-4.0), waits=[t4])
        tok_ba2 = OP("vector", lambda e: e.tensor_scalar_mul(out=sb["ba2"][:, :], in0=sb["vba"][:, :], scalar1=0.5), waits=[tok_mis])
        tok_bx2 = OP("vector", lambda e: e.tensor_scalar_mul(out=sb["bx2"][:, :], in0=sb["vbx"][:, :], scalar1=0.5), waits=[tok_mis])

        pt = OP("vector", lambda e: e.tensor_copy(out=sb["phi"][:, :], in_=sb["vth"][:, :]), waits=[tok_mis])
        for _ in range(6):
            m1 = OP("vector", lambda e: e.tensor_single_scalar(out=u1[:, :], in_=sb["phi"][:, :], scalar=PI, op=ALU.is_gt), waits=[pt])
            p1 = OP("vector", lambda e: e.scalar_tensor_tensor(out=sb["phi"][:, :], in0=u1[:, :], scalar=-TWO_PI_HI, in1=sb["phi"][:, :], op0=ALU.mult, op1=ALU.add), waits=[m1])
            p2 = OP("vector", lambda e: e.scalar_tensor_tensor(out=sb["phi"][:, :], in0=u1[:, :], scalar=-TWO_PI_LO, in1=sb["phi"][:, :], op0=ALU.mult, op1=ALU.add), waits=[p1])
            m2 = OP("vector", lambda e: e.tensor_single_scalar(out=u1[:, :], in_=sb["phi"][:, :], scalar=-PI, op=ALU.is_lt), waits=[p2])
            p3 = OP("vector", lambda e: e.scalar_tensor_tensor(out=sb["phi"][:, :], in0=u1[:, :], scalar=TWO_PI_HI, in1=sb["phi"][:, :], op0=ALU.mult, op1=ALU.add), waits=[m2])
            pt = OP("vector", lambda e: e.scalar_tensor_tensor(out=sb["phi"][:, :], in0=u1[:, :], scalar=TWO_PI_LO, in1=sb["phi"][:, :], op0=ALU.mult, op1=ALU.add), waits=[p3])
        mh = OP("vector", lambda e: e.tensor_single_scalar(out=u1[:, :], in_=sb["phi"][:, :], scalar=HPI, op=ALU.is_gt), waits=[pt])
        ml = OP("vector", lambda e: e.tensor_single_scalar(out=u2[:, :], in_=sb["phi"][:, :], scalar=-HPI, op=ALU.is_lt), waits=[mh])
        r1 = OP("vector", lambda e: e.scalar_tensor_tensor(out=sb["phi"][:, :], in0=u1[:, :], scalar=-PI_HI, in1=sb["phi"][:, :], op0=ALU.mult, op1=ALU.add), waits=[ml])
        r2 = OP("vector", lambda e: e.scalar_tensor_tensor(out=sb["phi"][:, :], in0=u1[:, :], scalar=-PI_LO, in1=sb["phi"][:, :], op0=ALU.mult, op1=ALU.add), waits=[r1])
        r3 = OP("vector", lambda e: e.scalar_tensor_tensor(out=sb["phi"][:, :], in0=u2[:, :], scalar=PI_HI, in1=sb["phi"][:, :], op0=ALU.mult, op1=ALU.add), waits=[r2])
        tok_phi = OP("vector", lambda e: e.scalar_tensor_tensor(out=sb["phi"][:, :], in0=u2[:, :], scalar=PI_LO, in1=sb["phi"][:, :], op0=ALU.mult, op1=ALU.add), waits=[r3])
        s_sin = OP("scalar", lambda e: e.activation(out=u1[:, :], in_=sb["phi"][:, :], func=AF.Sin), waits=[tok_phi])
        c1 = OP("vector", lambda e: e.tensor_scalar_add(out=u2[:, :], in0=sb["phi"][:, :], scalar1=HPI_HI), waits=[tok_phi])
        c2 = OP("vector", lambda e: e.tensor_scalar_add(out=u2[:, :], in0=u2[:, :], scalar1=HPI_LO), waits=[c1])
        s_cos = OP("scalar", lambda e: e.activation(out=u3[:, :], in_=u2[:, :], func=AF.Sin), waits=[c2])
        v1 = OP("vector", lambda e: e.tensor_mul(out=u2[:, :], in0=u3[:, :], in1=sb["vwr"][:, :]), waits=[s_cos])
        v2 = OP("vector", lambda e: e.tensor_mul(out=u4[:, :], in0=u1[:, :], in1=sb["vwi"][:, :]), waits=[s_sin, v1])
        v3 = OP("vector", lambda e: e.tensor_add(out=u2[:, :], in0=u2[:, :], in1=u4[:, :]), waits=[v2])
        v4 = OP("vector", lambda e: e.tensor_mul(out=u2[:, :], in0=u2[:, :], in1=sb["vwi"][:, :]), waits=[v3])
        tok_V = OP("vector", lambda e: e.tensor_scalar_mul(out=sb["V"][:, :], in0=u2[:, :], scalar1=0.1), waits=[v4])
        b11 = OP("vector", lambda e: e.tensor_scalar_mul(out=u4[:, :], in0=sb["vb1"][:, :], scalar1=0.1), waits=[v2])
        tok_sinb = OP("vector", lambda e: e.tensor_add(out=sb["sinb"][:, :], in0=sb["phi"][:, :], in1=u4[:, :]), waits=[b11])


    else:
        t_ones = tok_coef = tok_ba2 = tok_bx2 = tok_V = tok_sinb = None

    def chunk_bt(i):
        return i // (S // T), (i % (S // T)) * T

    def emit_xt_slice(i, dc, free_tok):
        b, t0 = chunk_bt(i)
        src = d_xT[b, dc * 128:(dc + 1) * 128, t0:t0 + T]
        dst = sb["xt"][:, dc * T:(dc + 1) * T]
        ring["xt_ready"][dc] = DMA(
            "gpsimd", lambda e, s2=src, d2=dst: e.dma_start(out=d2, in_=s2),
            "xt", waits=[free_tok])

    def emit_xk_loads(i):
        b, t0 = chunk_bt(i)
        for h in range(2):
            src = d_x[b, t0 + h * 256:t0 + (h + 1) * 256, :]
            src = src.rearrange("(tt p) d2 -> p tt d2", p=128)
            dst = sb["xk"][:, h * 2048:(h + 1) * 2048]
            dst = dst.rearrange("p (tt d2) -> p tt d2", d2=D)
            ring["xk_ready"][h] = DMA(
                "gpsimd", lambda e, s2=src, d2=dst: e.dma_start(out=d2, in_=s2),
                f"xk{h}", waits=[ring["xkslot"][h]])

    def emit_w1_load(g):
        if g >= NCH * FT:
            return
        ft = g % FT
        sl = g % 5
        src = d_w1s[ft].rearrange("(dc p) f -> p dc f", p=128)
        dst = sb["w1b"][:, sl * 1024:(sl + 1) * 1024]
        dst = dst.rearrange("p (dc f) -> p dc f", f=128)
        waits = [w1_cons.get(g - 5)] if g >= 5 else list(tok_w1s)
        DMA("sync", lambda e, s2=src, d2=dst: e.dma_start(out=d2, in_=s2),
            f"w1_{sl}", waits=waits)

    def emit_w2_load(g):
        if g >= NCH * 2 * FT:
            return
        fc = g % FT
        sl = g % 5
        src = d_w2s[fc * 128:(fc + 1) * 128, :]
        dst = sb["w2b"][:, sl * 1024:(sl + 1) * 1024]
        waits = [w2_cons.get(g - 5)] if g >= 5 else list(tok_w2s)
        DMA("sync", lambda e, s2=src, d2=dst: e.dma_start(out=d2, in_=s2),
            f"w2_{sl}", waits=waits)

    def emit_stats_pre(i):
        sq_toks = []
        rd_toks = []
        for tt in range(TT):
            sl = tt // 2
            col = (tt % 2) * 1024
            for hh in range(2):
                xkap = sb["xk"][:, sl * 2048 + col + hh * 512:
                               sl * 2048 + col + (hh + 1) * 512]
                sq_toks.append(OP(
                    "scalar", lambda e, x2=xkap, tt=tt, hh=hh:
                    e.activation(out=sb["sqs"][:, :], in_=x2, func=AF.Square,
                                 accum_out=sb["ssq8"][:, hh * TT + tt:hh * TT + tt + 1]),
                    waits=[ring["xk_ready"][sl]]))
            xkap = sb["xk"][:, sl * 2048 + col: sl * 2048 + col + 1024]
            rd_toks.append(OP(
                "vector", lambda e, x2=xkap, tt=tt:
                e.reduce_sum(out=sb["ssum"][:, tt:tt + 1], in_=x2, axis=AX.X),
                waits=[ring["xk_ready"][sl]]))
        ring["xkslot"][0] = [rd_toks[1], sq_toks[3]]
        ring["xkslot"][1] = [rd_toks[3], sq_toks[7]]
        mc = OP("vector", lambda e: e.tensor_add(
            out=sb["ssq"][:, :], in0=sb["ssq8"][:, 0:TT], in1=sb["ssq8"][:, TT:2 * TT]),
            waits=[sq_toks[-1]])
        m1 = OP("vector", lambda e: e.tensor_scalar_mul(out=sb["mean"][:, :], in0=sb["ssum"][:, :], scalar1=1.0 / D),
                waits=[rd_toks[-1]])
        m2 = OP("vector", lambda e: e.tensor_mul(out=sb["var"][:, :], in0=sb["mean"][:, :], in1=sb["mean"][:, :]),
                waits=[m1])
        m3 = OP("vector", lambda e: e.scalar_tensor_tensor(out=sb["var2"][:, :], in0=sb["ssq"][:, :], scalar=1.0 / D, in1=sb["var"][:, :], op0=ALU.mult, op1=ALU.subtract),
                waits=[m2, mc])
        m4 = OP("vector", lambda e: e.tensor_scalar_add(out=sb["var2"][:, :], in0=sb["var2"][:, :], scalar1=EPS),
                waits=[m3])
        return {"var": m4}

    def emit_stats_post_a(toks):
        a_std = OP("scalar", lambda e: e.activation(out=sb["std"][:, :], in_=sb["var2"][:, :], func=AF.Sqrt),
                   waits=[toks["var"]])
        d_rs = OP("vector", lambda e: e.reciprocal(out=sb["rstd"][:, :], in_=sb["std"][:, :]),
                  waits=[a_std])
        d_mr = OP("vector", lambda e: e.tensor_mul(out=sb["mr"][:, :], in0=sb["mean"][:, :], in1=sb["rstd"][:, :]),
                  waits=[d_rs])
        return {"rs": d_rs, "mr": d_mr}

    def emit_stats_post_b(pt):
        d_rs, d_mr = pt["rs"], pt["mr"]
        tr1 = []
        for tt in range(TT):
            tr1.append(OP(
                "tensor", lambda e, tt=tt:
                e.transpose(out=PS_RT[0:1, tt * 128:(tt + 1) * 128],
                            in_=sb["rstd"][:, tt:tt + 1], identity=sb["iden"][:, :]),
                waits=[d_rs, ring["psrt"][0]]))
        sr1 = []
        for tt in range(TT):
            sr1.append(OP(
                "vector", lambda e, tt=tt:
                e.tensor_copy(out=sb["strow"][0:1, tt * 128:(tt + 1) * 128],
                              in_=PS_RT[0:1, tt * 128:(tt + 1) * 128]),
                waits=[tr1[TT - 1]]))
        bc1 = OP("tensor", lambda e: e.matmul(out=PS_RT[:, :], lhsT=sb["ones"][:, :], rhs=sb["strow"][0:1, 0:512], start=True, stop=True),
                 waits=sr1 + [t_ones])
        e1 = OP("vector", lambda e: e.tensor_copy(out=sb["a"][:, 0:T], in_=PS_RT[:, :]),
                waits=[bc1, ring["a"][0]])
        tr2 = []
        for tt in range(TT):
            tr2.append(OP(
                "tensor", lambda e, tt=tt:
                e.transpose(out=PS_IT[0:1, tt * 128:(tt + 1) * 128],
                            in_=sb["mr"][:, tt:tt + 1], identity=sb["iden"][:, :]),
                waits=[d_mr, ring["psit"][0]]))
        sr2 = []
        for tt in range(TT):
            sr2.append(OP(
                "vector", lambda e, tt=tt:
                e.tensor_copy(out=sb["strow"][0:1, tt * 128:(tt + 1) * 128],
                              in_=PS_IT[0:1, tt * 128:(tt + 1) * 128]),
                waits=[tr2[TT - 1], bc1]))
        bc2 = OP("tensor", lambda e: e.matmul(out=PS_IT[:, :], lhsT=sb["ones"][:, :], rhs=sb["strow"][0:1, 0:512], start=True, stop=True),
                 waits=sr2)
        e2 = OP("vector", lambda e: e.tensor_copy(out=sb["a"][:, T:2 * T], in_=PS_IT[:, :]),
                waits=[bc2, ring["a"][1]])
        ring["psrt"][0] = e1
        ring["psit"][0] = e2
        return e1, e2

    def emit_stats_post(i, toks):
        return emit_stats_post_b(emit_stats_post_a(toks))

    prev = {
        "xn": None, "combine": [None] * DC, "scan": [None] * DC,
        "carry": [None] * DC, "silu_bank": [None] * 4,
        "ffn1_last_pe": None, "ffn2B": {}, "pshfree": [None, None],
    }

    def emit_xn_dc(i, dc, bc_toks):
        bc1, bc2 = bc_toks
        xt_ap = sb["xt"][:, dc * T:(dc + 1) * T]
        w0 = [ring["xt_ready"][dc], bc1, bc2, ring["xn"][dc], prev["scan"][dc]]
        o1 = OP("vector", lambda e, x2=xt_ap, dc=dc:
                e.tensor_mul(out=sb["xn32"][:, dc * T:(dc + 1) * T], in0=x2, in1=sb["a"][:, 0:T]),
                waits=w0)
        o2 = OP("vector", lambda e, dc=dc:
                e.tensor_sub(out=sb["xn32"][:, dc * T:(dc + 1) * T],
                             in0=sb["xn32"][:, dc * T:(dc + 1) * T], in1=sb["a"][:, T:2 * T]),
                waits=[o1])
        o3 = OP("vector", lambda e, dc=dc:
                e.tensor_copy(out=sb["xnbf"][:, dc * T:(dc + 1) * T],
                              in_=sb["xn32"][:, dc * T:(dc + 1) * T]),
                waits=[o2])
        return (o1, o2, o3)

    if SUB & 4:
        for g in range(5):
            emit_w1_load(g)
            emit_w2_load(g)
    if SUB & 2:
        st0 = emit_stats_pre(0)
    if SUB & 16:
        bc0 = emit_stats_post(0, st0)
    if SUB & 32:
        _xn0 = [emit_xn_dc(0, dc, bc0) for dc in range(DC)]
        prev["xn"] = [(o2, o3) for (o1, o2, o3) in _xn0]
        ring["psrt"][0] = _xn0[-1][0]
        ring["psit"][0] = _xn0[-1][1]

    for i in range(NCH):
        b, t0 = chunk_bt(i)
        batch_start = (i % (S // T)) == 0

        if i + 1 < NCH:
            emit_xk_loads(i + 1)

        tok_a = [None] * DC
        stn = None
        for half in (range(2) if SUB & 64 else []):
            dts = list(range(half * 4, half * 4 + 4))
            tok_it = {}
            tok_cl = {}
            for dt in dts:
                asl = dt % 4
                tokr = None
                for dc in range(DC):
                    last = dc == DC - 1
                    wdc = [prev["xn"][dc][1]]
                    if dc == 0:
                        wdc += [ring["psrt"][0], tok_wa_done]
                    tokr = OP("tensor", lambda e, dc=dc, dt=dt, last=last:
                              e.matmul(out=PS_RT[:, :],
                                       lhsT=sb["wa"][:, dc * 1024 + dt * 128: dc * 1024 + (dt + 1) * 128],
                                       rhs=sb["xnbf"][:, dc * T:(dc + 1) * T],
                                       start=(dc == 0), stop=last),
                              waits=wdc, inc=last)
                pe_rt = tokr
                toki = None
                for dc in range(DC):
                    last = dc == DC - 1
                    toki = OP("tensor", lambda e, dc=dc, dt=dt, last=last:
                              e.matmul(out=PS_IT[:, :],
                                       lhsT=sb["wx"][:, dc * 1024 + dt * 128: dc * 1024 + (dt + 1) * 128],
                                       rhs=sb["xnbf"][:, dc * T:(dc + 1) * T],
                                       start=(dc == 0), stop=last),
                              waits=([ring["psit"][0]] if dc == 0 else ()), inc=last)
                pe_it = toki
                a_th = OP("scalar", lambda e, dt=dt:
                          e.activation(out=sb["sqs"][:, :], in_=PS_RT[:, :], func=AF.Tanh,
                                       scale=0.5, bias=sb["ba2"][:, dt:dt + 1]),
                          waits=[pe_rt, tok_ba2, tok_coef])
                ring["psrt"][0] = a_th
                a_a = OP("scalar", lambda e, dt=dt, asl=asl:
                         e.activation(out=sb["a"][:, asl * T:(asl + 1) * T], in_=sb["sqs"][:, :],
                                      func=AF.Exp, scale=sb["coefh"][:, dt:dt + 1],
                                      bias=sb["coefh"][:, dt:dt + 1]),
                         waits=[a_th, ring["a"][asl]])
                tok_a[dt] = a_a
                a_ti = OP("scalar", lambda e, dt=dt, asl=asl:
                          e.activation(out=sb["it"][:, asl * T:(asl + 1) * T], in_=PS_IT[:, :],
                                       func=AF.Tanh, scale=0.5, bias=sb["bx2"][:, dt:dt + 1]),
                          waits=[pe_it, tok_bx2, ring["it"][asl]])
                ring["psit"][0] = a_ti
                a_sq = OP("scalar", lambda e, asl=asl:
                          e.activation(out=sb["asq"][:, asl * T:(asl + 1) * T],
                                       in_=sb["a"][:, asl * T:(asl + 1) * T], func=AF.Square),
                          waits=[a_a, ring["asq"][asl]])
                d_it = OP("vector", lambda e, asl=asl:
                          e.tensor_scalar(out=sb["it"][:, asl * T:(asl + 1) * T],
                                          in0=sb["it"][:, asl * T:(asl + 1) * T],
                                          scalar1=0.5, scalar2=0.5, op0=ALU.mult, op1=ALU.add),
                          waits=[a_ti])
                tok_it[dt] = d_it
                tok_cl[dt] = OP("vector", lambda e, asl=asl:
                                e.tensor_single_scalar(out=sb["asq"][:, asl * T:(asl + 1) * T],
                                                       in_=sb["asq"][:, asl * T:(asl + 1) * T],
                                                       scalar=1.0, op=ALU.min),
                                waits=[a_sq])

            if half == 1 and i + 1 < NCH:
                stn = emit_stats_pre(i + 1)

            tok_s = {}
            for dt in dts:
                asl = dt % 4
                tok_s[dt] = OP("scalar", lambda e, asl=asl:
                               e.activation(out=sb["sq"][:, asl * T:(asl + 1) * T],
                                            in_=sb["asq"][:, asl * T:(asl + 1) * T],
                                            func=AF.Sqrt, scale=-1.0, bias=1.0),
                               waits=[tok_cl[dt], ring["sq"][asl]])
            if half == 1 and i + 1 < NCH:
                stn_pt = emit_stats_post_a(stn)

            for dt in dts:
                asl = dt % 4
                d_t2 = OP("vector", lambda e, asl=asl, dt=dt:
                          e.tensor_mul(out=sb["asq"][:, asl * T:(asl + 1) * T],
                                       in0=sb["sq"][:, asl * T:(asl + 1) * T],
                                       in1=sb["xn32"][:, dt * T:(dt + 1) * T]),
                          waits=[tok_s[dt], prev["xn"][dt][0]])
                ring["sq"][asl] = d_t2
                d_b = OP("vector", lambda e, asl=asl:
                         e.tensor_mul(out=sb["asq"][:, asl * T:(asl + 1) * T],
                                      in0=sb["it"][:, asl * T:(asl + 1) * T],
                                      in1=sb["asq"][:, asl * T:(asl + 1) * T]),
                         waits=[d_t2, tok_it[dt]])
                ring["it"][asl] = d_b
                init = 0.0 if batch_start else sb["carry"][:, dt:dt + 1]
                d_sc = OP("vector", lambda e, asl=asl, dt=dt, init=init:
                          e.tensor_tensor_scan(out=sb["s"][:, dt * T:(dt + 1) * T],
                                               data0=sb["a"][:, asl * T:(asl + 1) * T],
                                               data1=sb["asq"][:, asl * T:(asl + 1) * T],
                                               initial=init, op0=ALU.mult, op1=ALU.add),
                          waits=[d_b, tok_a[dt], prev["combine"][dt], prev["carry"][dt]])
                ring["a"][asl] = d_sc
                ring["asq"][asl] = d_sc
                prev["scan"][dt] = d_sc
                prev["carry"][dt] = OP(
                    "vector", lambda e, dt=dt:
                    e.tensor_copy(out=sb["carry"][:, dt:dt + 1],
                                  in_=sb["s"][:, dt * T + T - 1: dt * T + T]),
                    waits=[d_sc])

        if (SUB & 64) and i + 1 < NCH:
            bcn = emit_stats_post_b(stn_pt)
        if not (SUB & 64) and i + 1 < NCH:
            stn = emit_stats_pre(i + 1)
            bcn = emit_stats_post(i + 1, stn)

        if not (SUB & 128):
            prev["ffn1_last_pe"] = None
            tok_act = [None] * FT
            sin_toks = [None] * FT
            q_toks = [None] * FT
        sin_toks = [None] * FT
        q_toks = [None] * FT
        tok_act = [None] * FT
        for ft in (range(FT) if SUB & 128 else []):
            g = i * FT + ft
            sl = g % 5
            psh = PS_H0 if ft % 2 == 0 else PS_H1
            w1_ready = (f"w1_{sl}", (g // 5 + 1) * 16)
            if ft >= 2:
                bankfree = [sin_toks[ft - 2], q_toks[ft - 2]]
            else:
                bankfree = prev["pshfree"][ft % 2] or []
            tokh = None
            for dc in range(DC):
                last = dc == DC - 1
                wdc = [prev["xn"][dc][1]]
                if dc == 0:
                    wdc = [w1_ready] + list(bankfree) + wdc
                tokh = OP("tensor", lambda e, dc=dc, sl=sl, psh=psh, last=last:
                          e.matmul(out=psh[:, :],
                                   lhsT=sb["w1b"][:, sl * 1024 + dc * 128: sl * 1024 + (dc + 1) * 128],
                                   rhs=sb["xnbf"][:, dc * T:(dc + 1) * T],
                                   start=(dc == 0), stop=last),
                          waits=wdc, inc=last)
            w1_cons[g] = tokh
            emit_w1_load(g + 5)
            s1sl = ft % 2
            a_sin = OP("scalar", lambda e, ft=ft, psh=psh, s1sl=s1sl:
                       e.activation(out=sb["s1"][:, s1sl * T:(s1sl + 1) * T], in_=psh[:, :],
                                    func=AF.Sin, scale=0.1, bias=sb["sinb"][:, ft:ft + 1]),
                       waits=[tokh, tok_sinb, ring["s1"][s1sl]])
            sin_toks[ft] = a_sin
            d_q = OP("vector", lambda e, psh=psh, ft=ft, s1sl=s1sl:
                     e.scalar_tensor_tensor(out=sb["q"][:, s1sl * T:(s1sl + 1) * T],
                                            in0=sb["s1"][:, s1sl * T:(s1sl + 1) * T],
                                            scalar=sb["V"][:, ft:ft + 1], in1=psh[:, :],
                                            op0=ALU.mult, op1=ALU.add),
                     waits=[a_sin, tok_V, ring["q"][s1sl]])
            q_toks[ft] = d_q
            ring["s1"][s1sl] = d_q
            d_act = OP("vector", lambda e, ft=ft, s1sl=s1sl:
                       e.tensor_scalar(out=sb["act"][:, ft * T:(ft + 1) * T],
                                       in0=sb["q"][:, s1sl * T:(s1sl + 1) * T],
                                       scalar1=sb["vb1"][:, ft:ft + 1], scalar2=0.0,
                                       op0=ALU.add, op1=ALU.max),
                       waits=[d_q, prev["ffn2B"].get(ft)])
            ring["q"][s1sl] = d_act
            tok_act[ft] = d_act
        prev["pshfree"] = [[sin_toks[FT - 2], q_toks[FT - 2]],
                           [sin_toks[FT - 1], q_toks[FT - 1]]]
        prev["ffn1_last_pe"] = w1_cons[i * FT + FT - 1]

        for dc in range(DC):
            ring["xn"][dc] = prev["ffn1_last_pe"]

        new_ffn2B = {}
        nxt_xn = [None] * DC
        for pas in (range(2) if SUB & 256 else []):
            pef = None
            for fc in range(FT):
                g2 = i * 2 * FT + pas * FT + fc
                sl = g2 % 5
                w2_ready = (f"w2_{sl}", (g2 // 5 + 1) * 16)
                w0 = [w2_ready, tok_act[fc]]
                if fc == 0:
                    w0 = w0 + [prev["silu_bank"][j] for j in range(4)]
                lastmm = None
                for j in range(4):
                    dt = pas * 4 + j
                    inc = (j == 3)
                    lastmm = OP("tensor", lambda e, j=j, sl=sl, dt=dt, fc=fc:
                                e.matmul(out=PS_F[j][:, :],
                                         lhsT=sb["w2b"][:, sl * 1024 + dt * 128: sl * 1024 + (dt + 1) * 128],
                                         rhs=sb["act"][:, fc * T:(fc + 1) * T],
                                         start=(fc == 0), stop=(fc == FT - 1)),
                                waits=(w0 if j == 0 else ()), inc=inc)
                w2_cons[g2] = lastmm
                emit_w2_load(g2 + 5)
                if pas == 1:
                    new_ffn2B[fc] = lastmm
                if fc == FT - 1:
                    pef = lastmm
            for j in range(4):
                dt = pas * 4 + j
                qsl = j % 2
                a_g = OP("scalar", lambda e, dt=dt, qsl=qsl, j=j:
                         e.activation(out=sb["q"][:, qsl * T:(qsl + 1) * T], in_=PS_F[j][:, :],
                                      func=AF.Silu, bias=sb["vb2"][:, dt:dt + 1]),
                         waits=[pef, ring["q"][qsl]])
                prev["silu_bank"][j] = a_g
                d_p = OP("vector", lambda e, dt=dt, qsl=qsl:
                         e.tensor_mul(out=sb["q"][:, qsl * T:(qsl + 1) * T],
                                      in0=sb["q"][:, qsl * T:(qsl + 1) * T],
                                      in1=sb["s"][:, dt * T:(dt + 1) * T]),
                         waits=[a_g, prev["scan"][dt]])
                osl = dt % 2
                ou_occ[osl] += 1
                xt_ap = sb["xt"][:, dt * T:(dt + 1) * T]
                d_o = OP("vector", lambda e, qsl=qsl, osl=osl, x2=xt_ap:
                         e.tensor_add(out=sb["outb"][:, osl * T:(osl + 1) * T],
                                      in0=sb["q"][:, qsl * T:(qsl + 1) * T], in1=x2),
                         waits=[d_p, (f"ou{osl}", (ou_occ[osl] - 1) * 16),
                                ring["xt_ready"][dt]])
                ring["q"][qsl] = d_o
                prev["combine"][dt] = d_o
                dst = d_out[b, dt * 128:(dt + 1) * 128, t0:t0 + T]
                DMA("gpsimd", lambda e, d2=dst, osl=osl:
                    e.dma_start(out=d2, in_=sb["outb"][:, osl * T:(osl + 1) * T]),
                    f"ou{osl}", waits=[d_o])
                if i + 1 < NCH:
                    emit_xt_slice(i + 1, dt, d_o)
                    nxt_xn[dt] = emit_xn_dc(i + 1, dt, bcn)
        prev["ffn2B"] = new_ffn2B
        if (SUB & 256) and i + 1 < NCH:
            prev["xn"] = [(o2, o3) for (o1, o2, o3) in nxt_xn]
            ring["a"][0] = nxt_xn[DC - 1][0]
            ring["a"][1] = nxt_xn[DC - 1][1]

    for slx in range(2):
        W("gpsimd", (f"ou{slx}", dcnt[f"ou{slx}"]))

    with stack:
        with nc.Block() as block:
            @block.sync
            def _(e):
                for f in prog["sync"]:
                    f(e)

            @block.tensor
            def _(e):
                for f in prog["tensor"]:
                    f(e)

            @block.vector
            def _(e):
                for f in prog["vector"]:
                    f(e)

            @block.scalar
            def _(e):
                for f in prog["scalar"]:
                    f(e)

            @block.gpsimd
            def _(e):
                for f in prog["gpsimd"]:
                    f(e)

    return nc


_NC_CACHE = None


def kernel(**inputs):
    global _NC_CACHE
    x = np.asarray(inputs["L0_attn_x"], dtype=np.float32)
    wa = np.asarray(inputs["L0_attn_Wa"], dtype=np.float32)
    wx = np.asarray(inputs["L0_attn_Wx"], dtype=np.float32)
    ba = np.asarray(inputs["L0_attn_ba"], dtype=np.float32)
    bx = np.asarray(inputs["L0_attn_bx"], dtype=np.float32)
    lam = np.asarray(inputs["L0_attn_Lam"], dtype=np.float32)
    w1 = np.asarray(inputs["L0_ffn_W1"], dtype=np.float32)
    b1 = np.asarray(inputs["L0_ffn_b1"], dtype=np.float32)
    th = np.asarray(inputs["L0_ffn_quantum_phases"], dtype=np.float32)
    wr = np.asarray(inputs["L0_ffn_quantum_weights_real"], dtype=np.float32)
    wi = np.asarray(inputs["L0_ffn_quantum_weights_imag"], dtype=np.float32)
    w2 = np.asarray(inputs["L0_ffn_W2"], dtype=np.float32)
    b2 = np.asarray(inputs["L0_ffn_b2"], dtype=np.float32)

    def vpack(v, k):
        return np.ascontiguousarray(v.reshape(k, 128).T)

    common = {
        "waT": np.ascontiguousarray(wa.T), "wxT": np.ascontiguousarray(wx.T),
        "w1": np.ascontiguousarray(w1), "w2": np.ascontiguousarray(w2),
        "vba": vpack(ba, DC), "vbx": vpack(bx, DC), "vb2": vpack(b2, DC),
        "vlam": vpack(lam, DC), "vb1": vpack(b1, FT), "vth": vpack(th, FT),
        "vwr": vpack(wr, FT), "vwi": vpack(wi, FT),
        "iden": np.eye(128, dtype=np.float32),
    }
    in_maps = []
    for c in range(NCORE):
        xc = x[c * BC:(c + 1) * BC]
        m = dict(common)
        m["x"] = np.ascontiguousarray(xc)
        m["xT"] = np.ascontiguousarray(xc.transpose(0, 2, 1))
        in_maps.append(m)

    if _NC_CACHE is None:
        _NC_CACHE = build_program()
    res = run_bass_kernel_spmd(_NC_CACHE, in_maps, core_ids=list(range(NCORE)))
    out = np.empty((B, S, D), dtype=np.float32)
    for c in range(NCORE):
        out[c * BC:(c + 1) * BC] = np.asarray(
            res.results[c]["outT"], dtype=np.float32).transpose(0, 2, 1)
    return out


if __name__ == "__main__":
    rng = np.random.default_rng(0)
    ins = {
        "L0_attn_x": rng.standard_normal((B, S, D)).astype(np.float32),
        "L0_attn_Wa": (rng.standard_normal((D, D)) / 32).astype(np.float32),
        "L0_attn_Wx": (rng.standard_normal((D, D)) / 32).astype(np.float32),
        "L0_attn_ba": (rng.standard_normal(D) * 0.02).astype(np.float32),
        "L0_attn_bx": (rng.standard_normal(D) * 0.02).astype(np.float32),
        "L0_attn_Lam": (4.0 + 5 * rng.random(D)).astype(np.float32),
        "L0_ffn_W1": (rng.standard_normal((D, FF)) / 32).astype(np.float32),
        "L0_ffn_b1": (rng.standard_normal(FF) * 0.2).astype(np.float32),
        "L0_ffn_quantum_phases": (rng.standard_normal(FF) * 6.28).astype(np.float32),
        "L0_ffn_quantum_weights_real": (rng.standard_normal(FF) * 0.1).astype(np.float32),
        "L0_ffn_quantum_weights_imag": (rng.standard_normal(FF) * 0.1).astype(np.float32),
        "L0_ffn_W2": (rng.standard_normal((FF, D)) / 64).astype(np.float32),
        "L0_ffn_b2": (rng.standard_normal(D) * 0.2).astype(np.float32),
    }
    out = kernel(**ins)
    print("ran", out.shape, out.dtype, float(np.abs(out).mean()))
